# version 03 (final)
# baseline (speedup 1.0000x reference)
import math
import numpy as np
import concourse.bass as bass
import concourse.mybir as mybir

F32 = mybir.dt.float32
BF16 = mybir.dt.bfloat16
I32 = mybir.dt.int32
U8 = mybir.dt.uint8
AF = mybir.ActivationFunctionType
ALU = mybir.AluOpType
AX = mybir.AxisListType

T = 2048
D = 1024
NT = 16
ND = 8
TB = 512
NTB = 4
PST = 256
NH = 16
HK = 64
CH = 128
NCH = 16
INV2PI = 1.0 / (2.0 * math.pi)
TWOPI = 2.0 * math.pi

ENGS = ("v", "s", "t", "g", "q")


class Ctx:
    def __init__(self, en, eng, nc, sems, bufs, cfg):
        self.en = en
        self.e = eng
        self.nc = nc
        self.sem = sems
        self.buf = bufs
        self.cfg = cfg
        self.c = {}
        self.last = {}
        self.nsince = {k: 0 for k in ENGS}
        self._updated = {}

    def _emit(self, en, method, *args, **kw):
        if self.en == en:
            inst = getattr(self.e, method)(*args, **kw)
            self.last[en] = inst
        else:
            inst = None
        self.nsince[en] += 1
        return inst

    def v(self, m, *a, **k):
        return self._emit("v", m, *a, **k)

    def s(self, m, *a, **k):
        return self._emit("s", m, *a, **k)

    def t(self, m, *a, **k):
        return self._emit("t", m, *a, **k)

    def g(self, m, *a, **k):
        return self._emit("g", m, *a, **k)

    def q(self, m, *a, **k):
        return self._emit("q", m, *a, **k)

    def inc(self, inst, sem, n=1):
        self.c[sem.name] = self.c.get(sem.name, 0) + n
        if inst is not None:
            assert id(inst) not in self._updated, "instruction already has sem update"
            inst.then_inc(sem, n)
            self._updated[id(inst)] = inst

    def val(self, sem):
        return self.c.get(sem.name, 0)

    def wait(self, en, sem, v=None):
        if self.en == en:
            self.e.wait_ge(sem, v if v is not None else self.val(sem))

    def dma(self, qen, out, in_, **kw):
        sem = {"q": self.sem["qd"], "g": self.sem["gd"], "s": self.sem["sd"]}[qen]
        if self.en == qen:
            inst = self.e.dma_start(out=out, in_=in_, **kw)
        else:
            inst = None
        self.inc(inst, sem, 16)
        return self.val(sem)

    def wait_dma(self, en, qen, v=None):
        sem = {"q": self.sem["qd"], "g": self.sem["gd"], "s": self.sem["sd"]}[qen]
        self.wait(en, sem, v if v is not None else self.val(sem))

    def mark(self, en):
        if self.nsince[en] == 0:
            return
        sem = self.sem[en]
        self.c[sem.name] = self.c.get(sem.name, 0) + 1
        if self.en == en:
            inst = self.last.get(en)
            if inst is not None and id(inst) not in self._updated:
                inst.then_inc(sem, 1)
                self._updated[id(inst)] = inst
            else:
                self.e.drain()
                self.e.sem_inc(sem, 1)
        self.nsince[en] = 0

    def vsync(self):
        self.mark("v")
        self.wait("v", self.sem["v"])

    def ssync(self):
        self.mark("s")
        self.wait("s", self.sem["s"])

    def barrier(self, producers, waiters):
        for p in producers:
            self.mark(p)
        for p in producers:
            if p in ("qd", "gd", "sd"):
                continue
            for w in waiters:
                if w != p:
                    self.wait(w, self.sem[p])


def ceil_div(a, b):
    return (a + b - 1) // b


class Alloc:

    def __init__(self, th, ncols):
        self.th = th
        self.n = ncols
        self.pos = 0
        self.names = {}

    def take(self, name, cols):
        assert self.pos + cols <= self.n, (name, self.pos, cols, self.n)
        off = self.pos
        self.pos += cols
        self.names[name] = (off, cols)
        return off

    def ap(self, name, p0=0, p1=128, c0=0, c1=None):
        off, cols = self.names[name]
        if c1 is None:
            c1 = cols
        return self.th[p0:p1, off + c0: off + c1]


def build(cfg):
    nb = cfg["nb"]
    nc = bass.Bass("TRN2", target_bir_lowering=False, debug=False)
    io = {}

    xin = nc.dram_tensor("xin", [nb, T, D], F32, kind="ExternalInput").ap()
    xout = nc.dram_tensor("xout", [nb, T, D], F32, kind="ExternalOutput").ap()

    W = {}

    def win(name, shape):
        W[name] = nc.dram_tensor(name, shape, F32, kind="ExternalInput").ap()
        return W[name]

    if cfg.get("do_l0attn", True):
        for pfx in ("fwd", "bwd"):
            win(f"L0_attn_{pfx}_log_real", [PST])
            win(f"L0_attn_{pfx}_imag", [PST])
            win(f"L0_attn_{pfx}_B_real", [PST, D])
            win(f"L0_attn_{pfx}_B_imag", [PST, D])
            win(f"L0_attn_{pfx}_C_real", [D, PST])
            win(f"L0_attn_{pfx}_C_imag", [D, PST])
            win(f"L0_attn_{pfx}_D", [D])
            win(f"L0_attn_{pfx}_log_Delta", [PST])
    if cfg.get("do_l0ffn", True):
        win("L0_ffn_router_w", [32, D])
        win("L0_ffn_router_bias", [32])
        win("L0_ffn_gate_w", [32, 128, D])
        win("L0_ffn_up_w", [32, 128, D])
        win("L0_ffn_down_w", [32, D, 128])
        win("L0_ffn_sgate_w", [512, D])
        win("L0_ffn_sup_w", [512, D])
        win("L0_ffn_sdown_w", [D, 512])
    if cfg.get("do_l1", True):
        for nm, shp in [("q_w", [D, D]), ("q_b", [D]), ("k_w", [D, D]), ("k_b", [D]),
                        ("v_w", [D, D]), ("v_b", [D]), ("o_w", [D, D]), ("o_b", [D]),
                        ("ln1_w", [D]), ("ln1_b", [D]), ("ln2_w", [D]), ("ln2_b", [D]),
                        ("dec_w", [D, NH]), ("dec_b", [NH]),
                        ("f1_w", [D, 4096]), ("f1_b", [4096]),
                        ("f2_w", [4096, D]), ("f2_b", [D])]:
            win(f"L1_attn_{nm}", shp)
        win("L1_ffn_W_experts", [8, D, D])
        win("L1_ffn_b_experts", [8, D])
        win("L1_ffn_W_gate", [8, D])
        win("L1_ffn_b_gate", [8])

    SC = {}

    def scr(name, shape, dt=BF16):
        SC[name] = nc.dram_tensor(name, shape, dt, kind="Internal").ap()
        return SC[name]

    if cfg.get("do_l0attn", True):
        scr("sBbarT", [2, 2, D, PST])
        scr("sCT", [2, 2, PST, D])
        scr("sTab", [2, 2, 2, 128, T], F32)
    if cfg.get("do_l0ffn", True):
        scr("wscr", [32, T], F32)
        scr("sGateT", [32, D, 128])
        scr("sUpT", [32, D, 128])
        scr("sDownT", [32, 128, D])
        scr("sSgT", [D, 512])
        scr("sSuT", [D, 512])
        scr("sSdT", [512, D])
        scr("sRouterT", [D, 32])
    if cfg.get("do_l1", True):
        scr("sQW", [D, D]); scr("sKW", [D, D]); scr("sVW", [D, D]); scr("sOW", [D, D])
        scr("sDecW", [D, NH])
        scr("sF1", [D, 4096]); scr("sF2", [4096, D])
        scr("sWexpT", [8, D, D])
        scr("sWgT", [D, 8])
        scr("sVB", [D]); scr("sOB", [D]); scr("sBExp", [8, D])
        scr("cscr", [NH, T], F32)
        scr("cbtscr", [NCH, NH], F32)
        scr("scscr", [NH, NCH], F32)

    dumps = cfg.get("dumps", {})
    DT = {}
    for name, shape in dumps.items():
        DT[name] = nc.dram_tensor("dbg_" + name, list(shape), F32,
                                  kind="ExternalOutput").ap()

    from contextlib import ExitStack
    with ExitStack() as st:
        E = st.enter_context
        XTM = E(nc.sbuf_tensor("XTM", [128, NT * D], BF16))
        BUF1 = E(nc.sbuf_tensor("BUF1", [128, ND * T], BF16))
        BUF2 = E(nc.sbuf_tensor("BUF2", [128, ND * T], BF16))
        BUF3 = E(nc.sbuf_tensor("BUF3", [128, ND * T], BF16))
        AR = E(nc.sbuf_tensor("AR", [128, 6 * T], F32))
        SMT = E(nc.sbuf_tensor("SMT", [128, 4352], F32))
        WB = E(nc.sbuf_tensor("WB", [128, 6656], BF16))
        ONESB = E(nc.sbuf_tensor("ONESB", [1, 128], BF16))
        PA = E(nc.psum_tensor("PA", [128, 2048], F32))
        PB = E(nc.psum_tensor("PB", [128, 1024], F32))
        PBF = E(nc.psum_tensor("PBF", [128, 1024], BF16))
        PBF2 = E(nc.psum_tensor("PBF2", [128, 1024], BF16))
        IDB = E(nc.sbuf_tensor("IDB", [128, 128], BF16))
        IDF = E(nc.sbuf_tensor("IDF", [128, 128], F32))
        ONESC = E(nc.sbuf_tensor("ONESC", [128, 8], BF16))

        sems = {}
        for snm in ("qd", "gd", "sd", "v", "s", "t", "g", "q", "pe", "pes", "tp"):
            sems[snm] = E(nc.semaphore(snm))

        block = E(nc.Block())

        bufs = dict(XTM=XTM, BUF1=BUF1, BUF2=BUF2, BUF3=BUF3, AR=AR, SMT=SMT,
                    WB=WB, ONESB=ONESB, PA=PA, PB=PB, PBF=PBF, PBF2=PBF2,
                    IDB=IDB, IDF=IDF, ONESC=ONESC,
                    W=W, SC=SC, DT=DT, xin=xin, xout=xout)

        def run(en, eng):
            ctx = Ctx(en, eng, nc, sems, bufs, cfg)
            program(ctx)

        @block.vector
        def _(e):
            run("v", e)

        @block.scalar
        def _(e):
            run("s", e)

        @block.tensor
        def _(e):
            run("t", e)

        @block.gpsimd
        def _(e):
            run("g", e)

        @block.sync
        def _(e):
            run("q", e)

    return nc



def program(ctx):
    cfg = ctx.cfg
    pa_init(ctx)
    smalloc_static(ctx)
    if cfg.get("do_l0attn", True):
        prep_l0attn(ctx)
    if cfg.get("do_l0ffn", True):
        prep_l0ffn(ctx)
    if cfg.get("do_l1", True):
        prep_l1(ctx)
    ctx.barrier(("v", "s", "t", "g"), ("v", "s", "t", "g", "q"))
    for b in range(cfg["nb"]):
        batch(ctx, b)


def smalloc_static(ctx):
    sm = Alloc(ctx.buf["SMT"], 4352)
    sm.take("rho", 4)
    sm.take("theta", 4)
    sm.take("Dcol", 16)
    sm.take("wre", 4)
    sm.take("wim", 4)
    sm.take("nwim", 4)
    sm.take("ptmp", 16)
    sm.take("lnm", 16)
    sm.take("lnr", 16)
    sm.take("lnm2", 16)
    sm.take("lnr2", 16)
    sm.take("wtm", 512)
    sm.take("wrow", 1024)
    sm.take("ones", 2)
    sm.take("trash", 2048)
    sm.take("colt", 64)
    sm.take("ln1w", 8)
    sm.take("ln1b", 8)
    sm.take("ln2w", 8)
    sm.take("ln2b", 8)
    sm.take("qb", 8)
    sm.take("kb", 8)
    sm.take("f1b", 32)
    sm.take("f2b", 8)
    sm.take("decb", 1)
    sm.take("bgate", 1)
    sm.take("onesrowf", 128)
    sm.take("attT", 144)
    sm.take("zero", 1)
    sm.take("wtm2", 128)
    sm.take("obc", 8)
    sm.take("eps1", 1)
    sm.take("dbgA", 8)
    ctx.sm = sm
    ctx.v("memset", sm.ap("ones"), 1.0)
    ctx.v("memset", ctx.buf["ONESB"][:, :], 1.0)
    ctx.v("memset", sm.ap("eps1"), 1e-5)
    ctx.v("memset", sm.ap("onesrowf"), 1.0)
    ctx.v("memset", ctx.buf["ONESC"][:, :], 1.0)
    ctx.v("memset", sm.ap("zero"), 0.0)
    ctx.mark("v")
    make_idents(ctx)


def dump(ctx, name, src_ap):
    if name not in ctx.buf["DT"]:
        return
    ctx.barrier(("v", "s", "t"), ("g",))
    ctx.wait_dma("g", "q")
    dst = ctx.buf["DT"][name]
    ctx.dma("g", dst, src_ap)
    ctx.wait_dma("g", "g")
    for en2 in ("v", "s", "t", "q"):
        ctx.wait_dma(en2, "g")



def pa_init(ctx):
    ctx.pa_use = 0
    ctx.pbf_use = 0
    ctx.pa_rec = []


def pa_acquire(ctx):
    m = ctx.pa_use
    ctx.pa_use += 1
    if m >= 4:
        assert len(ctx.pa_rec) > m - 4, "psum release for prior slice use not emitted yet"
        snm, val = ctx.pa_rec[m - 4]
        ctx.wait("t", ctx.sem[snm], val)
    return ctx.buf["PA"][:, (m % 4) * 512:(m % 4 + 1) * 512]


def pa_release(ctx, inst, en="v"):
    snm = "pe" if en == "v" else "pes"
    ctx.inc(inst, ctx.sem[snm], 1)
    ctx.pa_rec.append((snm, ctx.val(ctx.sem[snm])))


def pbf_acquire(ctx):
    m = ctx.pbf_use
    ctx.pbf_use += 1
    if m >= 2:
        ctx.wait("t", ctx.sem["tp"], m - 1)
    return ctx.buf["PBF"] if m % 2 == 0 else ctx.buf["PBF2"]


def pbf_release(ctx, inst):
    ctx.inc(inst, ctx.sem["tp"], 1)


def make_idents(ctx):
    sm = ctx.sm
    IDB, IDF = ctx.buf["IDB"], ctx.buf["IDF"]
    rowt = ctx.buf["AR"][:, 0:128]
    colt = ctx.buf["AR"][:, 128:256]
    ctx.g("iota", rowt, pattern=[[1, 128]], base=0, channel_multiplier=0,
          allow_small_or_imprecise_dtypes=True)
    ctx.g("iota", colt, pattern=[[0, 128]], base=0, channel_multiplier=1,
          allow_small_or_imprecise_dtypes=True)
    ctx.mark("g")
    ctx.wait("v", ctx.sem["g"])
    dump(ctx, "rowt0", rowt)
    dump(ctx, "coltb", colt)
    ctx.v("tensor_tensor", IDF[:, :], rowt, colt, ALU.is_equal)
    ctx.vsync()
    ctx.v("tensor_copy", IDB[:, :], IDF[:, :])
    ctx.mark("v")
    ctx.wait("t", ctx.sem["v"])
    dump(ctx, "idf0", IDF[:, :])


def transpose_tile(ctx, psum_ap, src_ap, bf16=True):
    k = src_ap.shape[0]
    b0 = src_ap.base_partition()
    ident = ctx.buf["IDB"] if bf16 else ctx.buf["IDF"]
    return ctx.t("transpose", psum_ap, src_ap, ident[b0:b0 + k, b0:b0 + k])



def prep_transpose(ctx, src, dst, R, C, negate=False):
    WB = ctx.buf["WB"]
    nI, nJ = R // 128, C // 128
    assert C <= 1024 and R <= 1024
    ctx.barrier(("t", "v"), ("g", "v"))
    ctx.wait_dma("g", "q")
    ctx.wait_dma("v", "q")
    for i in range(nI):
        ib = WB[:, (i % 2) * 1024:(i % 2) * 1024 + C]
        ob = WB[:, 2048 + (i % 2) * 1024: 2048 + (i % 2) * 1024 + C]
        if i >= 2:
            ctx.wait("g", ctx.sem["t"])
            ctx.wait_dma("v", "q")
        ctx.dma("g", ib, src[i * 128:(i + 1) * 128, :])
        ctx.wait_dma("t", "g")
        pb = pbf_acquire(ctx)
        ti = None
        for j in range(nJ):
            ti = transpose_tile(ctx, pb[:, j * 128:(j + 1) * 128],
                                ib[:, j * 128:(j + 1) * 128])
        ctx.mark("t")
        ctx.wait("v", ctx.sem["t"])
        if negate:
            ev = ctx.v("tensor_scalar_mul", ob, pb[:, 0:C], -1.0)
        else:
            ev = ctx.v("tensor_copy", ob, pb[:, 0:C])
        pbf_release(ctx, ev)
        ctx.mark("v")
        ctx.wait("q", ctx.sem["v"])
        dslice = dst[:, i * 128:(i + 1) * 128].rearrange("(j cp) rr -> cp j rr", cp=128)
        ctx.dma("q", dslice, ob.rearrange("p (j rr) -> p j rr", rr=128))
    ctx.barrier(("v", "t"), ("g", "q"))
    ctx.wait_dma("q", "q")


def prep_cast(ctx, src, dst):
    ctx.dma("g", dst, src)



def _round_wrap(ctx, out_ap, in_ap, it_ap):
    ctx.v("tensor_copy", it_ap, in_ap)
    ctx.vsync()
    ctx.v("tensor_copy", out_ap, it_ap)
    ctx.vsync()
    ctx.v("tensor_sub", out_ap, in_ap, out_ap)
    ctx.vsync()


def prep_l0attn(ctx):
    sm = ctx.sm
    W, SC = ctx.buf["W"], ctx.buf["SC"]
    AR = ctx.buf["AR"]
    IT = ctx.buf["AR"][:, 3 * T:4 * T].bitcast(I32)
    for di, pfx in enumerate(("fwd", "bwd")):
        base = 0
        names = ["log_real", "imag", "log_Delta"]
        for ni, nmm in enumerate(names):
            src = W[f"L0_attn_{pfx}_{nmm}"].rearrange("(pt p) -> pt p", p=128).unsqueeze(2)
            for pt in range(2):
                ctx.dma("q", sm.ap("ptmp", c0=ni * 2 + pt, c1=ni * 2 + pt + 1), src[pt])
        for dt in range(8):
            ctx.dma("q", sm.ap("Dcol", c0=di * 8 + dt, c1=di * 8 + dt + 1),
                    W[f"L0_attn_{pfx}_D"][dt * 128:(dt + 1) * 128].unsqueeze(1))
        ctx.wait_dma("v", "q")
        ctx.wait_dma("s", "q")
        for pt in range(2):
            lr = sm.ap("ptmp", c0=0 + pt, c1=1 + pt)
            im = sm.ap("ptmp", c0=2 + pt, c1=3 + pt)
            ld = sm.ap("ptmp", c0=4 + pt, c1=5 + pt)
            t6 = sm.ap("ptmp", c0=6, c1=7)
            t7 = sm.ap("ptmp", c0=7, c1=8)
            t8 = sm.ap("ptmp", c0=8, c1=9)
            t9 = sm.ap("ptmp", c0=9, c1=10)
            t10 = sm.ap("ptmp", c0=10, c1=11)
            t11 = sm.ap("ptmp", c0=11, c1=12)
            t12 = sm.ap("ptmp", c0=12, c1=13)
            t13 = sm.ap("ptmp", c0=13, c1=14)
            rho = sm.ap("rho", c0=di * 2 + pt, c1=di * 2 + pt + 1)
            tht = sm.ap("theta", c0=di * 2 + pt, c1=di * 2 + pt + 1)
            ctx.s("activation", t6, ld, AF.Exp)
            ctx.s("activation", t7, lr, AF.Exp)
            ctx.mark("s"); ctx.wait("v", ctx.sem["s"])
            ctx.v("tensor_mul", tht, im, t6)
            ctx.v("tensor_mul", t8, t7, t6)
            ctx.vsync()
            ctx.mark("v"); ctx.wait("s", ctx.sem["v"])
            ctx.s("activation", rho, t8, AF.Exp, scale=-1.0)
            ctx.mark("s"); ctx.wait("v", ctx.sem["s"])
            ctx.v("tensor_single_scalar", t9, tht, INV2PI, ALU.mult)
            ctx.vsync()
            _round_wrap(ctx, t10, t9, IT[:, 0:1])
            ctx.vsync()
            ctx.v("tensor_single_scalar", t11, t9, 0.25, ALU.add)
            ctx.vsync()
            _round_wrap(ctx, t12, t11, IT[:, 0:2048][:, 0:1])
            ctx.vsync()
            ctx.mark("v"); ctx.wait("s", ctx.sem["v"])
            sinth = sm.ap("ptmp", c0=6, c1=7)
            costh = sm.ap("ptmp", c0=7, c1=8)
            ctx.s("activation", sinth, t10, AF.Sin, scale=TWOPI)
            ctx.s("activation", costh, t12, AF.Sin, scale=TWOPI)
            ctx.mark("s"); ctx.wait("v", ctx.sem["s"])
            lbr = t8; lbi = t9
            if di == 0 and pt == 0:
                ctx.v("tensor_copy", sm.ap("dbgA", c0=0, c1=1), sinth)
                ctx.v("tensor_copy", sm.ap("dbgA", c0=1, c1=2), costh)
            ctx.v("tensor_mul", lbr, rho, costh)
            ctx.v("tensor_mul", lbi, rho, sinth)
            if di == 0 and pt == 0:
                ctx.vsync()
                ctx.v("tensor_copy", sm.ap("dbgA", c0=2, c1=3), lbr)
                ctx.v("tensor_copy", sm.ap("dbgA", c0=3, c1=4), lbi)
                ctx.v("tensor_copy", sm.ap("dbgA", c0=4, c1=5), t10)
                ctx.v("tensor_copy", sm.ap("dbgA", c0=5, c1=6), t12)
            ctx.vsync()
            lamre = t10
            ctx.s("activation", lamre, lr, AF.Exp)
            ctx.mark("s"); ctx.wait("v", ctx.sem["s"])
            m2 = t11
            ctx.v("tensor_mul", t12, im, im)
            ctx.v("tensor_mul", m2, lamre, lamre)
            ctx.vsync()
            ctx.v("tensor_add", m2, m2, t12)
            ctx.vsync()
            mask = t12
            ctx.v("tensor_single_scalar", mask, m2, 1e-12, ALU.is_ge)
            ctx.vsync()
            sre = t13
            ctx.v("tensor_mul", sre, lamre, mask)
            ctx.vsync()
            ctx.v("tensor_scalar_mul", sre, sre, -1.0)
            one_m = sm.ap("ptmp", c0=14, c1=15)
            ctx.v("tensor_single_scalar", one_m, mask, 1.0, ALU.subtract)
            ctx.vsync()
            ctx.v("scalar_tensor_tensor", sre, one_m, -1e-6, sre, ALU.mult, ALU.add)
            sim = one_m
            ctx.v("tensor_mul", sim, im, mask)
            ctx.v("tensor_single_scalar", lbr, lbr, 1.0, ALU.subtract)
            ctx.vsync()
            den = sm.ap("ptmp", c0=15, c1=16)
            ctx.v("tensor_mul", den, sre, sre)
            tq = sm.ap("colt", c0=0, c1=1)
            ctx.v("tensor_mul", tq, sim, sim)
            ctx.vsync()
            ctx.v("tensor_add", den, den, tq)
            ctx.vsync()
            ctx.v("reciprocal", den, den)
            ctx.vsync()
            wre = sm.ap("wre", c0=di * 2 + pt, c1=di * 2 + pt + 1)
            wim = sm.ap("wim", c0=di * 2 + pt, c1=di * 2 + pt + 1)
            nwim = sm.ap("nwim", c0=di * 2 + pt, c1=di * 2 + pt + 1)
            ta = sm.ap("colt", c0=1, c1=2)
            tb_ = sm.ap("colt", c0=2, c1=3)
            ctx.v("tensor_mul", ta, lbr, sre)
            ctx.v("tensor_mul", tb_, lbi, sim)
            ctx.vsync()
            ctx.v("tensor_add", ta, ta, tb_)
            ctx.vsync()
            ctx.v("tensor_mul", wre, ta, den)
            ctx.v("tensor_mul", ta, lbi, sre)
            ctx.v("tensor_mul", tb_, lbr, sim)
            ctx.vsync()
            ctx.v("tensor_sub", ta, ta, tb_)
            ctx.vsync()
            ctx.v("tensor_mul", wim, ta, den)
            ctx.vsync()
            ctx.v("tensor_scalar_mul", nwim, wim, -1.0)
            ctx.vsync()
            if di == 0 and pt == 0:
                dump(ctx, "ptmpd", sm.ap("ptmp", c1=16))
                dump(ctx, "dbgA", sm.ap("dbgA"))
            ctx.mark("v")

            ramp = AR[:, 0:2048]
            acol = AR[:, 2048:4096]
            fcol = AR[:, 4096:6144]
            ftmp = AR[:, 6144:8192]
            ctx.wait("g", ctx.sem["v"])
            ctx.g("iota", ramp, pattern=[[1, T]], base=0, channel_multiplier=0,
                  allow_small_or_imprecise_dtypes=True)
            ctx.mark("g"); ctx.wait("v", ctx.sem["g"])
            ctx.wait_dma("v", "q"); ctx.wait_dma("s", "q")
            ctx.v("tensor_scalar", acol, ramp, tht, INV2PI, ALU.mult, ALU.mult)
            _round_wrap(ctx, fcol, acol, IT[:, :])
            ctx.mark("v"); ctx.wait("s", ctx.sem["v"])
            sin_t = AR[:, 8192:10240]
            ctx.s("activation", sin_t, fcol, AF.Sin, scale=TWOPI)
            ctx.mark("s")
            ctx.wait("v", ctx.sem["s"])
            ctx.v("tensor_single_scalar", acol, acol, 0.25, ALU.add)
            _round_wrap(ctx, fcol, acol, IT[:, :])
            ctx.mark("v"); ctx.wait("s", ctx.sem["v"])
            cos_t = AR[:, 10240:12288]
            ctx.s("activation", cos_t, fcol, AF.Sin, scale=TWOPI)
            ctx.mark("s")
            ctx.wait("q", ctx.sem["s"])
            ctx.dma("q", SC["sTab"][di, pt, 0], cos_t)
            ctx.dma("q", SC["sTab"][di, pt, 1], sin_t)

        for pt in range(2):
            bre = AR[:, 0:1024]
            bim = AR[:, 1024:2048]
            src_re = W[f"L0_attn_{pfx}_B_real"][pt * 128:(pt + 1) * 128, :]
            src_im = W[f"L0_attn_{pfx}_B_imag"][pt * 128:(pt + 1) * 128, :]
            ctx.wait("q", ctx.sem["v"])
            ctx.dma("q", bre, src_re)
            ctx.dma("q", bim, src_im)
            ctx.wait_dma("v", "q")
            wre = sm.ap("wre", c0=di * 2 + pt, c1=di * 2 + pt + 1)
            wim = sm.ap("wim", c0=di * 2 + pt, c1=di * 2 + pt + 1)
            nwim = sm.ap("nwim", c0=di * 2 + pt, c1=di * 2 + pt + 1)
            bbr = ctx.buf["WB"][:, 0:1024]
            bbi = ctx.buf["WB"][:, 1024:2048]
            t1 = AR[:, 2048:3072]
            ctx.v("tensor_scalar", t1, bre, wre, None, ALU.mult)
            ctx.v("scalar_tensor_tensor", bbr, bim, nwim, t1, ALU.mult, ALU.add)
            ctx.v("tensor_scalar", t1, bre, wim, None, ALU.mult)
            ctx.v("scalar_tensor_tensor", bbi, bim, wre, t1, ALU.mult, ALU.add)
            ctx.mark("v")
            ctx.wait("t", ctx.sem["v"])
            for ci, bb in enumerate((bbr, bbi)):
                pb = pbf_acquire(ctx)
                for j in range(8):
                    transpose_tile(ctx, pb[:, j * 128:(j + 1) * 128],
                                   bb[:, j * 128:(j + 1) * 128])
                ctx.mark("t")
                ctx.wait("v", ctx.sem["t"])
                ob = ctx.buf["WB"][:, 2048 + ci * 1024: 2048 + ci * 1024 + 1024]
                ev = ctx.v("tensor_copy", ob, pb[:, :])
                pbf_release(ctx, ev)
                ctx.mark("v")
                ctx.wait("q", ctx.sem["v"])
                dsl = SC["sBbarT"][di, ci][:, pt * 128:(pt + 1) * 128]
                ctx.dma("q", dsl.rearrange("(j cp) rr -> cp j rr", cp=128),
                        ob.rearrange("p (j rr) -> p j rr", rr=128))
            ctx.wait_dma("v", "q")
        for ci, nmm in enumerate(("C_real", "C_imag")):
            prep_transpose(ctx, W[f"L0_attn_{pfx}_{nmm}"], SC["sCT"][di, ci],
                           R=D, C=PST, negate=(ci == 1))
    ctx.barrier(("v", "s", "t", "g"), ("v", "s", "t", "g", "q"))



def batch(ctx, b):
    ctx.barrier(("v", "s", "t", "g"), ("v", "s", "t", "g", "q"))
    ctx.wait_dma("v", "q"); ctx.wait_dma("s", "q"); ctx.wait_dma("t", "q")
    ctx.wait_dma("v", "g"); ctx.wait_dma("s", "g"); ctx.wait_dma("t", "g")
    ctx.wait_dma("q", "g"); ctx.wait_dma("g", "q")
    load_x(ctx, b)
    if b == 0 and "tabs_dram" in ctx.buf["DT"]:
        ctx.dma("g", ctx.buf["DT"]["tabs_dram"], ctx.buf["SC"]["sTab"][0, 0, 1])
        ctx.dma("g", ctx.buf["DT"]["tabc_dram"], ctx.buf["SC"]["sTab"][0, 0, 0])
        ctx.wait_dma("g", "g")
    if b == 0 and "sgt" in ctx.buf["DT"]:
        ctx.dma("g", ctx.buf["DT"]["sgt"], ctx.buf["SC"]["sSgT"])
        ctx.dma("g", ctx.buf["DT"]["sut"], ctx.buf["SC"]["sSuT"])
        ctx.wait_dma("g", "g")
    if b == 0 and "bbarT" in ctx.buf["DT"]:
        ctx.dma("g", ctx.buf["DT"]["bbarT"], ctx.buf["SC"]["sBbarT"][0, 0])
        ctx.dma("g", ctx.buf["DT"]["ctre"], ctx.buf["SC"]["sCT"][0, 0])
        ctx.dma("g", ctx.buf["DT"]["ctim"], ctx.buf["SC"]["sCT"][0, 1])
        ctx.wait_dma("g", "g")
    if b == 0:
        dump(ctx, "wcols", ctx.sm.ap("wre", c1=4))
        dump(ctx, "wcols2", ctx.sm.ap("wim", c1=4))
        dump(ctx, "rhoc", ctx.sm.ap("rho", c1=4))
        dump(ctx, "thetac", ctx.sm.ap("theta", c1=4))
    if b == 0:
        dump(ctx, "xtm", ctx.buf["XTM"][:, :])
    ln_to_fm(ctx, which="l0")
    if b == 0:
        dump(ctx, "idf", ctx.buf["IDF"][:, :])
        dump(ctx, "idb", ctx.buf["IDB"][:, :])
        dump(ctx, "stg15", ctx.sm.ap("trash").bitcast(BF16)[:, 1024:2048])
        dump(ctx, "colt0", ctx.sm.ap("colt", c1=33))
        dump(ctx, "lnst", ctx.sm.ap("lnm", c1=16))
        dump(ctx, "lnst2", ctx.sm.ap("lnr", c1=16))
        dump(ctx, "xnfm", ctx.buf["BUF1"][:, :])
    if ctx.cfg.get("do_l0attn", True):
        l0_attn(ctx, b)
        if b == 0:
            dump(ctx, "xsbf", ctx.buf["BUF3"][:, :])
            dump(ctx, "s0fm", ctx.buf["BUF2"][:, :])
    if ctx.cfg.get("do_l0ffn", True):
        l0_ffn(ctx, b)
        if b == 0:
            dump(ctx, "g0tm", ctx.buf["BUF3"][:, :])
            dump(ctx, "wtm0", ctx.sm.ap("wtm"))
    if ctx.cfg.get("do_l0attn", True) and ctx.cfg.get("do_l0ffn", True):
        gate_residual(ctx, b, layer=0)
        if b == 0:
            dump(ctx, "x1tm", ctx.buf["XTM"][:, :])
    if ctx.cfg.get("do_l1", True):
        ln_to_fm(ctx, which="l1")
        if b == 0:
            dump(ctx, "l1st", ctx.sm.ap("lnm", c1=16))
            dump(ctx, "l1st2", ctx.sm.ap("lnr", c1=16))
            dump(ctx, "l1st3", ctx.sm.ap("lnm2", c1=16))
            dump(ctx, "l1st4", ctx.sm.ap("lnr2", c1=16))
            dump(ctx, "xqfm", ctx.buf["BUF1"][:, :])
        l1_qkv_dec(ctx, b)
        if b == 0:
            dump(ctx, "phiq", ctx.buf["BUF2"][:, :])
            dump(ctx, "phik", ctx.buf["BUF3"][:, :])
            dump(ctx, "vtm", ctx.buf["AR"][:, 0:4 * T].bitcast(BF16))
            dump(ctx, "lionws", ctx.buf["AR"][:, 4 * T:6 * T])
        if ctx.cfg.get("l1_full", True):
            l1_lion(ctx, b)
            if b == 0:
                dump(ctx, "atttm", ctx.buf["BUF1"][:, :])
            if ctx.cfg.get("l1_stop_after_lion", False):
                store_x(ctx, b)
                return
            l1_post(ctx, b)
            if b == 0:
                dump(ctx, "s1fm", ctx.buf["BUF1"][:, :])
            l1_ffn(ctx, b)
            if b == 0:
                dump(ctx, "g1tm", ctx.buf["BUF2"][:, :])
                dump(ctx, "wtm2d", ctx.sm.ap("wtm2"))
                dump(ctx, "xn1d", ctx.buf["BUF3"][:, :])
            gate_residual(ctx, b, layer=1)
    store_x(ctx, b)


def load_x(ctx, b):
    XTM = ctx.buf["XTM"]
    ctx.barrier(("v", "s", "t"), ("g",))
    src = ctx.buf["xin"][b].rearrange("(tt p) d -> p tt d", p=128)
    ctx.dma("g", XTM[:, :].rearrange("p (tt d) -> p tt d", d=D), src)


def xtile(ctx, tt):
    return ctx.buf["XTM"][:, tt * D:(tt + 1) * D]


def ln_to_fm(ctx, which):
    sm = ctx.sm
    AR = ctx.buf["AR"]
    trash = sm.ap("trash")
    trash_bf = sm.ap("trash").bitcast(BF16)
    vars_ = sm.ap("colt", c0=0, c1=16)
    ss = sm.ap("colt", c0=16, c1=32)
    ctx.barrier(("v", "s", "t"), ("v", "s", "t"))
    ctx.wait_dma("v", "g")
    ctx.wait_dma("s", "g")
    for tt in range(NT):
        ctx.v("tensor_scalar", trash_bf[:, 0:1024], xtile(ctx, tt), 1.0 / D, 0.0,
              ALU.mult, ALU.add, accum_out=sm.ap("lnm", c0=tt, c1=tt + 1))
        ctx.s("activation", trash[:, 512:1024].bitcast(BF16), xtile(ctx, tt), AF.Square,
              accum_out=ss[:, tt:tt + 1])
    ctx.mark("v"); ctx.mark("s")
    ctx.wait("v", ctx.sem["s"])
    if "ssdump" in ctx.buf["DT"]:
        dump(ctx, "ssdump", ss)
    lnm_all = sm.ap("lnm")
    va = sm.ap("colt", c0=34, c1=50)
    vb = sm.ap("colt", c0=0, c1=16)
    ctx.v("tensor_single_scalar", va, ss, 1.0 / D, ALU.mult)
    ctx.v("tensor_mul", vb, lnm_all, lnm_all)
    ctx.vsync()
    ctx.v("tensor_sub", vars_, va, vb)
    ctx.mark("v"); ctx.wait("s", ctx.sem["v"])
    ctx.s("activation", vars_, vars_, AF.Sqrt, bias=sm.ap("eps1"))
    ctx.mark("s"); ctx.wait("v", ctx.sem["s"])
    ctx.v("reciprocal", sm.ap("lnr"), vars_)
    ctx.vsync()
    if which == "l1":
        ln_inner(ctx)
        return
    for tt in range(NT):
        stg = trash_bf[:, (tt % 2) * 1024:(tt % 2) * 1024 + 1024]
        if tt >= 2:
            ctx.wait("v", ctx.sem["t"])
        ctx.v("tensor_scalar", stg, xtile(ctx, tt),
              sm.ap("lnm", c0=tt, c1=tt + 1), sm.ap("lnr", c0=tt, c1=tt + 1),
              ALU.subtract, ALU.mult)
        ctx.mark("v"); ctx.wait("t", ctx.sem["v"])
        pb = pbf_acquire(ctx)
        for dt in range(ND):
            transpose_tile(ctx, pb[:, dt * 128:(dt + 1) * 128],
                           stg[:, dt * 128:(dt + 1) * 128])
        ctx.mark("t"); ctx.wait("v", ctx.sem["t"])
        ev = None
        for dt in range(ND):
            ev = ctx.v("tensor_copy",
                       ctx.buf["BUF1"][:, dt * T + tt * 128: dt * T + (tt + 1) * 128],
                       pb[:, dt * 128:(dt + 1) * 128])
        pbf_release(ctx, ev)
    ctx.mark("v")


def ln_inner(ctx):
    sm = ctx.sm
    trash_bf = sm.ap("trash").bitcast(BF16)
    vars2 = sm.ap("colt", c0=0, c1=16)
    ctx.v("tensor_mul", vars2, sm.ap("lnr"), sm.ap("lnr"))
    ctx.vsync()
    ctx.v("tensor_scalar", vars2, vars2, -1e-5, 1.0 + 1e-5, ALU.mult, ALU.add)
    ctx.vsync()
    ctx.mark("v"); ctx.wait("s", ctx.sem["v"])
    ctx.s("activation", vars2, vars2, AF.Sqrt)
    ctx.mark("s"); ctx.wait("v", ctx.sem["s"])
    ctx.v("reciprocal", sm.ap("lnr2"), vars2)
    ctx.vsync()
    for tt in range(NT):
        stg = trash_bf[:, (tt % 2) * 1024:(tt % 2) * 1024 + 1024]
        stg2 = trash_bf[:, 2048 + (tt % 2) * 1024: 2048 + (tt % 2) * 1024 + 1024]
        if tt >= 2:
            ctx.wait("v", ctx.sem["t"])
        ctx.v("tensor_scalar", stg, xtile(ctx, tt),
              sm.ap("lnm", c0=tt, c1=tt + 1), sm.ap("lnr", c0=tt, c1=tt + 1),
              ALU.subtract, ALU.mult)
        ctx.v("tensor_scalar", stg2, stg,
              sm.ap("lnr2", c0=tt, c1=tt + 1), None, ALU.mult)
        ctx.mark("v"); ctx.wait("t", ctx.sem["v"])
        pb = pbf_acquire(ctx)
        for dt in range(ND):
            transpose_tile(ctx, pb[:, dt * 128:(dt + 1) * 128],
                           stg2[:, dt * 128:(dt + 1) * 128])
        ctx.mark("t"); ctx.wait("v", ctx.sem["t"])
        ev = None
        for dt in range(ND):
            dst = ctx.buf["BUF1"][:, dt * T + tt * 128: dt * T + (tt + 1) * 128]
            ev = ctx.v("tensor_scalar", dst, pb[:, dt * 128:(dt + 1) * 128],
                       sm.ap("ln1w", c0=dt, c1=dt + 1),
                       sm.ap("ln1b", c0=dt, c1=dt + 1),
                       ALU.mult, ALU.add)
        pbf_release(ctx, ev)
    ctx.mark("v")


def store_x(ctx, b):
    ctx.barrier(("v", "s", "t"), ("g",))
    dst = ctx.buf["xout"][b].rearrange("(tt p) d -> p tt d", p=128)
    ctx.dma("g", dst, ctx.buf["XTM"][:, :].rearrange("p (tt d) -> p tt d", d=D))
    ctx.wait_dma("g", "g")


def l0_attn(ctx, b):
    sm = ctx.sm
    AR = ctx.buf["AR"]
    WB = ctx.buf["WB"]
    BUF1, BUF2, BUF3 = ctx.buf["BUF1"], ctx.buf["BUF2"], ctx.buf["BUF3"]
    SC = ctx.buf["SC"]
    S0, S1, S2, S3, S4, S5 = [AR[:, i * T:(i + 1) * T] for i in range(6)]
    for di in range(2):
        sgn = 1.0 if di == 0 else -1.0
        ctx.barrier(("v", "s", "t"), ("v", "s", "t", "q"))
        for comp in range(2):
            ctx.dma("q", WB[:, comp * 2048:(comp + 1) * 2048].rearrange(
                "p (k c) -> p k c", c=256),
                SC["sBbarT"][di, comp].rearrange("(k dp) c -> dp k c", dp=128))
        wv = ctx.val(ctx.sem["qd"])
        for pt in range(2):
            ctx.wait_dma("q", "qd") if False else None
            ctx.barrier(("v",), ("q",))
            ctx.dma("q", S4, SC["sTab"][di, pt, 0])
            ctx.dma("q", S5, SC["sTab"][di, pt, 1])
            tabv = ctx.val(ctx.sem["qd"])
            ctx.wait("t", ctx.sem["qd"], wv)
            for comp in range(2):
                dst = (S0, S1)[comp]
                for tb in range(NTB):
                    ps = pa_acquire(ctx)
                    for k in range(8):
                        ctx.t("matmul", ps,
                              WB[:, comp * 2048 + k * 256 + pt * 128:
                                 comp * 2048 + k * 256 + pt * 128 + 128],
                              BUF1[:, k * T + tb * TB: k * T + (tb + 1) * TB],
                              start=(k == 0), stop=(k == 7))
                    ctx.mark("t")
                    ctx.wait("v", ctx.sem["t"])
                    ev = ctx.v("tensor_copy", dst[:, tb * TB:(tb + 1) * TB], ps)
                    pa_release(ctx, ev)
            ctx.wait_dma("v", "q", tabv)
            if di == 0 and pt == 0 and b == 0:
                dump(ctx, "bu_re", S0)
                dump(ctx, "bu_im", S1)
                dump(ctx, "tabc", S4)
                dump(ctx, "tabs", S5)
            rho_bc = sm.ap("rho", c0=di * 2 + pt, c1=di * 2 + pt + 1).broadcast_to([128, T])
            ctx.v("tensor_mul", S2, S4, S0)
            ctx.v("tensor_mul", S3, S5, S1)
            ctx.v("tensor_tensor", S2, S2, S3, ALU.add if di == 0 else ALU.subtract)
            ctx.v("tensor_mul", S3, S4, S1)
            ctx.v("tensor_mul", S1, S5, S0)
            ctx.v("tensor_tensor", S3, S3, S1, ALU.subtract if di == 0 else ALU.add)
            if di == 0:
                ctx.v("tensor_tensor_scan", S0, rho_bc, S2, 0.0, ALU.mult, ALU.add)
                ctx.v("tensor_tensor_scan", S1, rho_bc, S3, 0.0, ALU.mult, ALU.add)
            else:
                ctx.v("tensor_tensor_scan", S0[:, ::-1], rho_bc, S2[:, ::-1], 0.0,
                      ALU.mult, ALU.add)
                ctx.v("tensor_tensor_scan", S1[:, ::-1], rho_bc, S3[:, ::-1], 0.0,
                      ALU.mult, ALU.add)
            xs_re = BUF3[:, (0 * 2 + pt) * T:(0 * 2 + pt + 1) * T]
            xs_im = BUF3[:, (1 * 2 + pt) * T:(1 * 2 + pt + 1) * T]
            ctx.v("tensor_mul", S2, S4, S0)
            ctx.v("tensor_mul", S3, S5, S1)
            ctx.v("tensor_tensor", xs_re, S2, S3, ALU.subtract if di == 0 else ALU.add)
            ctx.v("tensor_mul", S2, S4, S1)
            ctx.v("tensor_mul", S3, S5, S0)
            ctx.v("tensor_tensor", xs_im, S2, S3, ALU.add if di == 0 else ALU.subtract)
            ctx.mark("v")
            if di == 0 and pt == 0 and b == 0:
                dump(ctx, "zre", S0)
                dump(ctx, "xsre0", xs_re)
        ctx.barrier(("t",), ("q",))
        for comp in range(2):
            ctx.dma("q", WB[:, comp * 2048:(comp + 1) * 2048].rearrange(
                "p (pt d) -> p pt d", d=1024),
                SC["sCT"][di, comp].rearrange("(pt pp) d -> pp pt d", pp=128))
        cv = ctx.val(ctx.sem["qd"])
        ctx.wait("t", ctx.sem["v"])
        ctx.wait("t", ctx.sem["qd"], cv)
        for dt in range(ND):
            for tb in range(NTB):
                ps = pa_acquire(ctx)
                m = 0
                for comp in range(2):
                    for pt in range(2):
                        ctx.t("matmul", ps,
                              WB[:, comp * 2048 + pt * 1024 + dt * 128:
                                 comp * 2048 + pt * 1024 + dt * 128 + 128],
                              BUF3[:, (comp * 2 + pt) * T + tb * TB:
                                   (comp * 2 + pt) * T + (tb + 1) * TB],
                              start=(m == 0), stop=(m == 3))
                        m += 1
                ctx.mark("t")
                ctx.wait("v", ctx.sem["t"])
                tr = sm.ap("trash", c0=(tb % 2) * 512, c1=(tb % 2) * 512 + 512)
                tr2 = sm.ap("trash", c0=1024 + (tb % 2) * 512, c1=1024 + (tb % 2) * 512 + 512)
                ctx.wait("v", ctx.sem["s"])
                ev = ctx.v("scalar_tensor_tensor", tr,
                           BUF1[:, dt * T + tb * TB: dt * T + (tb + 1) * TB],
                           sm.ap("Dcol", c0=di * 8 + dt, c1=di * 8 + dt + 1),
                           ps, ALU.mult, ALU.add)
                pa_release(ctx, ev)
                ctx.mark("v")
                ctx.wait("s", ctx.sem["v"])
                dst = BUF2[:, dt * T + tb * TB: dt * T + (tb + 1) * TB]
                if di == 0:
                    ctx.s("activation", dst, tr, AF.Gelu)
                    ctx.mark("s")
                else:
                    ctx.s("activation", tr2, tr, AF.Gelu)
                    ctx.mark("s")
                    ctx.wait("v", ctx.sem["s"])
                    ctx.v("tensor_add", dst, dst, tr2)
                    ctx.mark("v")
    ctx.mark("v"); ctx.mark("s")


def l0_ffn(ctx, b):
    sm = ctx.sm
    AR = ctx.buf["AR"]
    WB = ctx.buf["WB"]
    BUF1, BUF3 = ctx.buf["BUF1"], ctx.buf["BUF3"]
    SC = ctx.buf["SC"]
    ONESB = ctx.buf["ONESB"]

    ctx.barrier(("v", "s", "t"), ("v", "s", "t", "q", "g"))
    ctx.dma("q", WB[:, 0:256].rearrange("p (k e) -> p k e", e=32),
            SC["sRouterT"].rearrange("(k dp) e -> dp k e", dp=128))
    ctx.dma("g", WB[0:1, 256:288], ctx.buf["W"]["L0_ffn_router_bias"][None, :])
    rv = ctx.val(ctx.sem["qd"]); gv = ctx.val(ctx.sem["gd"])
    ctx.wait("t", ctx.sem["qd"], rv)
    ctx.wait_dma("t", "g", gv)
    for tt in range(NT):
        ps = pa_acquire(ctx)
        ps32 = ps[:, 0:32]
        for k in range(ND):
            ctx.t("matmul", ps32, BUF1[:, k * T + tt * 128: k * T + (tt + 1) * 128],
                  WB[:, k * 32 + 0:  k * 32 + 32] if False else WB[:, 0:256].rearrange("p (k e) -> p k e", e=32)[:, k],
                  start=(k == 0), stop=False)
        ctx.t("matmul", ps32, ONESB[:, :], WB[0:1, 256:288], start=False, stop=True)
        ctx.mark("t")
        ctx.wait("v", ctx.sem["t"])
        S = sm.ap("trash", c0=0, c1=32)
        m1 = sm.ap("colt", c0=0, c1=1)
        negm = sm.ap("colt", c0=1, c1=2)
        ssum = sm.ap("colt", c0=2, c1=3)
        thr = sm.ap("colt", c0=3, c1=4)
        sum4 = sm.ap("colt", c0=4, c1=5)
        mk = sm.ap("trash", c0=32, c1=64)
        pw = sm.ap("trash", c0=64, c1=96)
        ev = ctx.v("tensor_copy", S, ps32)
        pa_release(ctx, ev)
        ctx.vsync()
        ctx.v("reduce_max", m1, S, axis=AX.X)
        ctx.vsync()
        ctx.v("tensor_scalar_mul", negm, m1, -1.0)
        ctx.mark("v"); ctx.wait("s", ctx.sem["v"])
        ctx.s("activation", S, S, AF.Exp, bias=negm)
        ctx.mark("s"); ctx.wait("v", ctx.sem["s"])
        ctx.v("reduce_sum", ssum, S, axis=AX.X)
        ctx.vsync()
        ctx.v("reciprocal", ssum, ssum)
        ctx.vsync()
        ctx.v("tensor_scalar", S, S, ssum, None, ALU.mult)
        ctx.vsync()
        ctx.v("reduce_max", m1, S, axis=AX.X)
        ctx.vsync()
        ctx.v("tensor_copy", sum4, m1)
        ctx.v("tensor_copy", pw, S)
        ctx.vsync()
        for it in range(3):
            ctx.v("tensor_scalar", mk, pw, m1, None, ALU.is_ge)
            ctx.vsync()
            ctx.v("scalar_tensor_tensor", pw, mk, -1e30, pw, ALU.mult, ALU.add)
            ctx.vsync()
            ctx.v("reduce_max", m1, pw, axis=AX.X)
            ctx.vsync()
            ctx.v("tensor_add", sum4, sum4, m1)
        ctx.vsync()
        ctx.v("tensor_scalar", mk, S, m1, None, ALU.is_ge)
        ctx.v("tensor_scalar_max", sum4, sum4, 1e-12)
        ctx.vsync()
        ctx.v("tensor_mul", pw, S, mk)
        ctx.v("reciprocal", sum4, sum4)
        ctx.vsync()
        ctx.v("tensor_scalar", sm.ap("wtm", c0=tt * 32, c1=(tt + 1) * 32),
              pw, sum4, None, ALU.mult)
    ctx.mark("v")
    ctx.wait("t", ctx.sem["v"])
    for tt in range(NT):
        ps = pa_acquire(ctx)
        ctx.t("transpose", ps[0:32, 0:128], sm.ap("wtm", c0=tt * 32, c1=(tt + 1) * 32),
              ctx.buf["IDF"][:, :])
        ctx.mark("t")
        ctx.wait("v", ctx.sem["t"])
        ev = ctx.v("tensor_copy", sm.ap("trash", p0=0, p1=32, c0=tt * 128, c1=(tt + 1) * 128),
                   ps[0:32, 0:128])
        pa_release(ctx, ev)
    ctx.mark("v")
    ctx.wait("q", ctx.sem["v"])
    ctx.dma("q", SC["wscr"], sm.ap("trash", p0=0, p1=32))
    wsv = ctx.val(ctx.sem["qd"])

    ctx.barrier(("v", "s", "t"), ("v", "s", "t", "q"))
    ctx.wait_dma("v", "q"); ctx.wait_dma("s", "q"); ctx.wait_dma("t", "q")
    ctx.wait_dma("q", "g")
    HS = AR[:, 4 * T:6 * T].bitcast(BF16)
    for ht in range(4):
        sl = 2048 * (ht % 2)
        if ht >= 2:
            ctx.wait("q", ctx.sem["t"])
        ctx.dma("q", WB[:, sl:sl + 1024].rearrange("p (k c) -> p k c", c=128),
                SC["sSgT"][:, ht * 128:(ht + 1) * 128].rearrange("(k dp) c -> dp k c", dp=128))
        ctx.dma("q", WB[:, sl + 1024:sl + 2048].rearrange("p (k c) -> p k c", c=128),
                SC["sSuT"][:, ht * 128:(ht + 1) * 128].rearrange("(k dp) c -> dp k c", dp=128))
        lv = ctx.val(ctx.sem["qd"])
        ctx.wait("t", ctx.sem["qd"], lv)
        for tb in range(NTB):
            gps = pa_acquire(ctx)
            for k in range(ND):
                ctx.t("matmul", gps, WB[:, sl + k * 128:sl + (k + 1) * 128],
                      BUF1[:, k * T + tb * TB: k * T + (tb + 1) * TB],
                      start=(k == 0), stop=(k == 7))
            ups = pa_acquire(ctx)
            for k in range(ND):
                ctx.t("matmul", ups, WB[:, sl + 1024 + k * 128:sl + 1024 + (k + 1) * 128],
                      BUF1[:, k * T + tb * TB: k * T + (tb + 1) * TB],
                      start=(k == 0), stop=(k == 7))
            ctx.mark("t")
            ctx.wait("s", ctx.sem["t"])
            tr = sm.ap("trash", c0=(tb % 2) * 512, c1=(tb % 2) * 512 + 512)
            ctx.wait("s", ctx.sem["v"])
            e1 = ctx.s("activation", tr, gps, AF.Silu)
            pa_release(ctx, e1, "s")
            ctx.mark("s"); ctx.wait("v", ctx.sem["s"])
            e2 = ctx.v("tensor_tensor", HS[:, ht * 2048 + tb * TB: ht * 2048 + (tb + 1) * TB],
                       tr, ups, ALU.mult)
            pa_release(ctx, e2)
            ctx.mark("v")
        ctx.mark("t")
    if b == 0:
        dump(ctx, "hsd", HS)
    ctx.barrier(("v", "s", "t"), ("v", "s", "t", "q"))
    ctx.dma("q", WB[:, 0:4096].rearrange("p (k c) -> p k c", c=1024),
            SC["sSdT"].rearrange("(k hp) c -> hp k c", hp=128))
    sdv = ctx.val(ctx.sem["qd"])
    ctx.wait("t", ctx.sem["qd"], sdv)
    for tt in range(NT):
        ps = pa_acquire(ctx)
        ps2 = pa_acquire(ctx)
        for k in range(4):
            ctx.t("matmul", ps, HS[:, k * 2048 + tt * 128: k * 2048 + (tt + 1) * 128],
                  WB[:, k * 1024 + 0: k * 1024 + 512],
                  start=(k == 0), stop=(k == 3))
        for k in range(4):
            ctx.t("matmul", ps2, HS[:, k * 2048 + tt * 128: k * 2048 + (tt + 1) * 128],
                  WB[:, k * 1024 + 512: k * 1024 + 1024],
                  start=(k == 0), stop=(k == 3))
        ctx.mark("t")
        ctx.wait("v", ctx.sem["t"])
        e1 = ctx.v("tensor_copy", BUF3[:, tt * D: tt * D + 512], ps)
        pa_release(ctx, e1)
        e2 = ctx.v("tensor_copy", BUF3[:, tt * D + 512: (tt + 1) * D], ps2)
        pa_release(ctx, e2)
    ctx.mark("v")

    if b == 0:
        dump(ctx, "gsh", BUF3[:, :])
    ctx.barrier(("v", "s", "t"), ("v", "s", "t", "q", "g"))
    ctx.wait_dma("g", "q", wsv)
    groups = [list(range(0, 12)), list(range(12, 24)), list(range(24, 32))]
    for grp in groups:
        HE = ctx.buf["AR"][:, :].bitcast(BF16)
        ctx.barrier(("t",), ("q", "g"))
        for li, e in enumerate(grp):
            sl = (li % 2) * 2048
            wrow_bf = sm.ap("wrow").bitcast(BF16)
            if li >= 1 or grp[0] != 0:
                ctx.wait("g", ctx.sem["t"])
            ctx.dma("g", wrow_bf[0:1, 0:T], SC["wscr"][e][None, :])
            if li >= 2:
                ctx.wait("q", ctx.sem["t"])
            ctx.dma("q", WB[:, sl:sl + 1024].rearrange("p (k c) -> p k c", c=128),
                    SC["sGateT"][e].rearrange("(k dp) c -> dp k c", dp=128))
            ctx.dma("q", WB[:, sl + 1024:sl + 2048].rearrange("p (k c) -> p k c", c=128),
                    SC["sUpT"][e].rearrange("(k dp) c -> dp k c", dp=128))
            lv = ctx.val(ctx.sem["qd"])
            gv2 = ctx.val(ctx.sem["gd"])
            ctx.wait("t", ctx.sem["qd"], lv)
            ctx.wait_dma("t", "g", gv2)
            for tb in range(NTB):
                gps = pa_acquire(ctx)
                for k in range(ND):
                    ctx.t("matmul", gps, WB[:, sl + k * 128:sl + (k + 1) * 128],
                          BUF1[:, k * T + tb * TB: k * T + (tb + 1) * TB],
                          start=(k == 0), stop=(k == 7))
                ups = pa_acquire(ctx)
                for k in range(ND):
                    ctx.t("matmul", ups, WB[:, sl + 1024 + k * 128:sl + 1024 + (k + 1) * 128],
                          BUF1[:, k * T + tb * TB: k * T + (tb + 1) * TB],
                          start=(k == 0), stop=(k == 7))
                wps = pa_acquire(ctx)
                ctx.t("matmul", wps, ONESB[:, :],
                      wrow_bf[0:1, tb * TB:(tb + 1) * TB],
                      start=True, stop=True)
                ctx.mark("t")
                ctx.wait("s", ctx.sem["t"])
                tr = sm.ap("trash", c0=(tb % 2) * 512, c1=(tb % 2) * 512 + 512)
                tr2 = sm.ap("trash", c0=1024 + (tb % 2) * 512, c1=1536 + (tb % 2) * 512)
                ctx.wait("s", ctx.sem["v"])
                e1 = ctx.s("activation", tr, gps, AF.Silu)
                pa_release(ctx, e1, "s")
                ctx.mark("s"); ctx.wait("v", ctx.sem["s"])
                e2 = ctx.v("tensor_tensor", tr2, tr, ups, ALU.mult)
                pa_release(ctx, e2)
                e3 = ctx.v("tensor_tensor", HE[:, li * T + tb * TB: li * T + (tb + 1) * TB],
                           tr2, wps, ALU.mult)
                pa_release(ctx, e3)
                ctx.mark("v")
        if b == 0 and grp[0] == 0:
            dump(ctx, "he0", HE)
        if b == 0 and grp[0] == 12:
            dump(ctx, "he1", HE)
        ctx.barrier(("v",), ("t",))
        for dh in range(2):
            ctx.barrier(("t",), ("q",))
            for li, e in enumerate(grp):
                ctx.dma("q", WB[:, li * 512:(li + 1) * 512],
                        SC["sDownT"][e][:, dh * 512:(dh + 1) * 512])
            lv = ctx.val(ctx.sem["qd"])
            ctx.wait("t", ctx.sem["qd"], lv)
            for tt in range(NT):
                ps = pa_acquire(ctx)
                for li in range(len(grp)):
                    ctx.t("matmul", ps, HE[:, li * T + tt * 128: li * T + (tt + 1) * 128],
                          WB[:, li * 512:(li + 1) * 512],
                          start=(li == 0), stop=(li == len(grp) - 1))
                ctx.mark("t")
                ctx.wait("v", ctx.sem["t"])
                dst = BUF3[:, tt * D + dh * 512: tt * D + (dh + 1) * 512]
                ev = ctx.v("tensor_tensor", dst, dst, ps, ALU.add)
                pa_release(ctx, ev)
            ctx.mark("v")
        if b == 0 and grp[0] == 0:
            dump(ctx, "g0grp0", BUF3[:, :])
        if b == 0 and grp[0] == 12:
            dump(ctx, "g0grp1", BUF3[:, :])
    ctx.mark("v")
    ctx.wait("s", ctx.sem["v"])
    for tt in range(NT):
        ctx.s("activation", BUF3[:, tt * D:(tt + 1) * D],
              BUF3[:, tt * D:(tt + 1) * D], AF.Silu)
    ctx.mark("v"); ctx.mark("s")


def gate_residual(ctx, b, layer):
    sm = ctx.sm
    XTM = ctx.buf["XTM"]
    BUF2 = ctx.buf["BUF2"] if layer == 0 else ctx.buf["BUF1"]
    BUF3 = ctx.buf["BUF3"] if layer == 0 else ctx.buf["BUF2"]
    ctx.barrier(("v", "s"), ("t",))
    trash_bf = sm.ap("trash").bitcast(BF16)
    for tt in range(NT):
        pb = pbf_acquire(ctx)
        for dt in range(ND):
            transpose_tile(ctx, pb[:, dt * 128:(dt + 1) * 128],
                           BUF2[:, dt * T + tt * 128: dt * T + (tt + 1) * 128])
        ctx.mark("t")
        ctx.wait("v", ctx.sem["t"])
        tr = trash_bf[:, (tt % 2) * 1024:(tt % 2) * 1024 + 1024]
        e1 = ctx.v("tensor_tensor", tr, pb[:, :], BUF3[:, tt * D:(tt + 1) * D], ALU.mult)
        pbf_release(ctx, e1)
        ctx.v("tensor_tensor", XTM[:, tt * D:(tt + 1) * D],
              XTM[:, tt * D:(tt + 1) * D], tr, ALU.add)
        ctx.mark("v")
    ctx.mark("v")


def prep_l0ffn(ctx):
    W, SC = ctx.buf["W"], ctx.buf["SC"]
    for e in range(32):
        prep_transpose(ctx, W["L0_ffn_gate_w"][e], SC["sGateT"][e], R=128, C=1024)
        prep_transpose(ctx, W["L0_ffn_up_w"][e], SC["sUpT"][e], R=128, C=1024)
        prep_transpose(ctx, W["L0_ffn_down_w"][e], SC["sDownT"][e], R=1024, C=128)
    prep_transpose(ctx, W["L0_ffn_sgate_w"], SC["sSgT"], R=512, C=1024)
    prep_transpose(ctx, W["L0_ffn_sup_w"], SC["sSuT"], R=512, C=1024)
    prep_transpose(ctx, W["L0_ffn_sdown_w"], SC["sSdT"], R=1024, C=512)
    WB = ctx.buf["WB"]
    ib = WB[0:32, 0:1024]
    ctx.barrier(("v", "s", "t"), ("g", "q"))
    ctx.dma("g", ib, W["L0_ffn_router_w"])
    ctx.wait_dma("t", "g")
    for j in range(8):
        pb = pbf_acquire(ctx)
        transpose_tile(ctx, pb[:, 0:32], ib[:, j * 128:(j + 1) * 128])
        ctx.mark("t")
        ctx.wait("v", ctx.sem["t"])
        ob = WB[:, 1024 + (j % 2) * 32: 1024 + (j % 2) * 32 + 32]
        if j >= 2:
            ctx.wait_dma("v", "q")
        ev = ctx.v("tensor_copy", ob, pb[:, 0:32])
        pbf_release(ctx, ev)
        ctx.mark("v")
        ctx.wait("q", ctx.sem["v"])
        ctx.dma("q", SC["sRouterT"][j * 128:(j + 1) * 128, :], ob)
    ctx.wait_dma("q", "q")
    ctx.barrier(("v", "s", "t", "g"), ("v", "s", "t", "g", "q"))


def pa_drain(ctx):
    for snm, val in ctx.pa_rec[-4:]:
        for en in ("t", "v", "s"):
            ctx.wait(en, ctx.sem[snm], val)
    while ctx.pa_use % 4 != 0:
        ctx.pa_rec.append(("pe", ctx.val(ctx.sem["pe"])))
        ctx.pa_use += 1


def pa_manual_release(ctx, inst, n=4, en="v"):
    snm = "pe" if en == "v" else "pes"
    ctx.inc(inst, ctx.sem[snm], n)
    for _ in range(n):
        ctx.pa_rec.append((snm, ctx.val(ctx.sem[snm])))
    ctx.pa_use += n


def load_colvec(ctx, name, src1d, n, base=0):
    for i in range(n):
        ctx.dma("q", ctx.sm.ap(name, c0=base + i, c1=base + i + 1),
                src1d[i * 128:(i + 1) * 128].unsqueeze(1))


def prep_l1(ctx):
    W, SC = ctx.buf["W"], ctx.buf["SC"]
    sm = ctx.sm
    for nm, key in [("sQW", "L1_attn_q_w"), ("sKW", "L1_attn_k_w"),
                    ("sVW", "L1_attn_v_w"), ("sOW", "L1_attn_o_w"),
                    ("sDecW", "L1_attn_dec_w"), ("sF1", "L1_attn_f1_w"),
                    ("sF2", "L1_attn_f2_w"), ("sVB", "L1_attn_v_b"),
                    ("sOB", "L1_attn_o_b"), ("sBExp", "L1_ffn_b_experts")]:
        ctx.dma("g", SC[nm], W[key])
    for e in range(8):
        prep_transpose(ctx, W["L1_ffn_W_experts"][e], SC["sWexpT"][e], R=D, C=D)
    WB = ctx.buf["WB"]
    ib = WB[0:8, 0:1024]
    ctx.barrier(("v", "s", "t"), ("g", "q"))
    ctx.wait_dma("g", "q")
    ctx.dma("g", ib, W["L1_ffn_W_gate"])
    ctx.wait_dma("t", "g")
    for j in range(8):
        pb = pbf_acquire(ctx)
        transpose_tile(ctx, pb[:, 0:8], ib[:, j * 128:(j + 1) * 128])
        ctx.mark("t")
        ctx.wait("v", ctx.sem["t"])
        ob = WB[:, 1024 + (j % 2) * 8: 1024 + (j % 2) * 8 + 8]
        ctx.wait_dma("v", "q")
        ev = ctx.v("tensor_copy", ob, pb[:, 0:8])
        pbf_release(ctx, ev)
        ctx.mark("v")
        ctx.wait("q", ctx.sem["v"])
        ctx.dma("q", SC["sWgT"][j * 128:(j + 1) * 128, :], ob)
    ctx.wait_dma("q", "q")
    load_colvec(ctx, "ln1w", W["L1_attn_ln1_w"], 8)
    load_colvec(ctx, "ln1b", W["L1_attn_ln1_b"], 8)
    load_colvec(ctx, "ln2w", W["L1_attn_ln2_w"], 8)
    load_colvec(ctx, "ln2b", W["L1_attn_ln2_b"], 8)
    load_colvec(ctx, "qb", W["L1_attn_q_b"], 8)
    load_colvec(ctx, "kb", W["L1_attn_k_b"], 8)
    load_colvec(ctx, "f1b", W["L1_attn_f1_b"], 32)
    load_colvec(ctx, "f2b", W["L1_attn_f2_b"], 8)
    load_colvec(ctx, "obc", W["L1_attn_o_b"], 8)
    ctx.dma("q", sm.ap("decb", p1=16), W["L1_attn_dec_b"].unsqueeze(1))
    ctx.dma("q", sm.ap("bgate", p1=8), W["L1_ffn_b_gate"].unsqueeze(1))
    ctx.barrier(("v", "s", "t", "g"), ("v", "s", "t", "g", "q"))
    ctx.wait_dma("v", "q"); ctx.wait_dma("s", "q"); ctx.wait_dma("t", "q")
    ctx.wait_dma("v", "g"); ctx.wait_dma("s", "g"); ctx.wait_dma("t", "g")



def l1_qkv_dec(ctx, b):
    sm = ctx.sm
    WB = ctx.buf["WB"]
    SC = ctx.buf["SC"]
    BUF1, BUF2, BUF3 = ctx.buf["BUF1"], ctx.buf["BUF2"], ctx.buf["BUF3"]
    AR = ctx.buf["AR"]
    VTM = AR[:, :].bitcast(BF16)
    ONESB = ctx.buf["ONESB"]
    ctx.barrier(("v", "s", "t"), ("v", "s", "t", "q", "g"))
    ctx.wait_dma("t", "g"); ctx.wait_dma("q", "g")

    def chunk_load(mat_ap, dh, cols=512):
        ctx.barrier(("t",), ("q",))
        ctx.dma("q", WB[:, 0:8 * cols].rearrange("p (k c) -> p k c", c=cols),
                mat_ap[:, dh * cols:(dh + 1) * cols].rearrange("(k dp) c -> dp k c", dp=128))
        lv = ctx.val(ctx.sem["qd"])
        ctx.wait("t", ctx.sem["qd"], lv)

    for mat, dst, bcol in [(SC["sQW"], BUF2, "qb"), (SC["sKW"], BUF3, "kb")]:
        for dh in range(2):
            chunk_load(mat, dh)
            for dl in range(4):
                dt = dh * 4 + dl
                for tb in range(NTB):
                    ps = pa_acquire(ctx)
                    for k in range(ND):
                        ctx.t("matmul", ps, WB[:, k * 512 + dl * 128: k * 512 + (dl + 1) * 128],
                              BUF1[:, k * T + tb * TB: k * T + (tb + 1) * TB],
                              start=(k == 0), stop=(k == 7))
                    ctx.mark("t")
                    tr = sm.ap("trash", c0=(tb % 2) * 512, c1=(tb % 2) * 512 + 512)
                    tr2 = sm.ap("trash", c0=1024 + (tb % 2) * 512, c1=1536 + (tb % 2) * 512)
                    ctx.wait("v", ctx.sem["t"])
                    ctx.wait("v", ctx.sem["s"])
                    e1 = ctx.v("tensor_scalar", tr, ps, sm.ap(bcol, c0=dt, c1=dt + 1),
                               None, ALU.add)
                    pa_release(ctx, e1)
                    ctx.v("tensor_scalar_min", tr2, tr, 0.0)
                    ctx.mark("v"); ctx.wait("s", ctx.sem["v"])
                    ctx.s("activation", tr2, tr2, AF.Exp)
                    ctx.mark("s"); ctx.wait("v", ctx.sem["s"])
                    ctx.v("scalar_tensor_tensor",
                          dst[:, dt * T + tb * TB: dt * T + (tb + 1) * TB],
                          tr, 0.0, tr2, ALU.max, ALU.add)
                    ctx.mark("v")
    for dh in range(2):
        chunk_load(SC["sVW"], dh)
        if dh == 0:
            ctx.dma("q", WB[0:1, 4096:5120], SC["sVB"][None, :])
        vbv = ctx.val(ctx.sem["qd"])
        ctx.wait("t", ctx.sem["qd"], vbv)
        for tt in range(NT):
            ps = pa_acquire(ctx)
            for k in range(ND):
                ctx.t("matmul", ps, BUF1[:, k * T + tt * 128: k * T + (tt + 1) * 128],
                      WB[:, k * 512:(k + 1) * 512], start=(k == 0), stop=False)
            ctx.t("matmul", ps, ONESB[:, :], WB[0:1, 4096 + dh * 512: 4608 + dh * 512],
                  start=False, stop=True)
            ctx.mark("t")
            ctx.wait("v", ctx.sem["t"])
            ev = ctx.v("tensor_copy", VTM[:, tt * 1024 + dh * 512: tt * 1024 + (dh + 1) * 512], ps)
            pa_release(ctx, ev)
    ctx.mark("v")
    ctx.barrier(("t",), ("q",))
    ctx.dma("q", WB[:, 0:128].rearrange("p (k c) -> p k c", c=16),
            SC["sDecW"].rearrange("(k dp) c -> dp k c", dp=128))
    lv = ctx.val(ctx.sem["qd"])
    ctx.wait("t", ctx.sem["qd"], lv)
    pa_drain(ctx)
    PA = ctx.buf["PA"]
    for tb in range(NTB):
        for k in range(ND):
            ctx.t("matmul", PA[0:16, tb * 512:(tb + 1) * 512],
                  WB[:, k * 16:(k + 1) * 16],
                  BUF1[:, k * T + tb * TB: k * T + (tb + 1) * TB],
                  start=(k == 0), stop=(k == 7))
    ctx.mark("t"); ctx.wait("s", ctx.sem["t"]); ctx.wait("v", ctx.sem["t"])
    zps = PA[0:16, 0:2048]
    ctrash = sm.ap("trash", p1=16, c0=0, c1=2048)
    ctx.v("tensor_scalar", ctrash, zps, sm.ap("decb", p1=16), None, ALU.add)
    ctx.mark("v"); ctx.wait("s", ctx.sem["v"])
    ctx.s("activation", ctrash, ctrash, AF.Sigmoid)
    ctx.mark("s"); ctx.wait("v", ctx.sem["s"])
    ctx.v("tensor_scalar_max", ctrash, ctrash, 1e-8)
    ctx.mark("v"); ctx.wait("s", ctx.sem["v"])
    ctx.s("activation", ctrash, ctrash, AF.Ln)
    ctx.mark("s"); ctx.wait("v", ctx.sem["s"])
    ctx.v("tensor_tensor_scan", zps, ctrash,
          sm.ap("zero", p1=16).broadcast_to([16, 2048]), 0.0, ALU.add, ALU.add)
    ctx.vsync()
    cp_ = ctx.v("tensor_copy", ctrash, zps)
    pa_manual_release(ctx, cp_, 4, "v")
    ctx.vsync()
    ctx.wait("q", ctx.sem["v"])
    ctx.dma("q", SC["cscr"], ctrash)
    LW = lion_ws(ctx)
    ctx.wait("t", ctx.sem["v"])
    for m in range(NCH):
        ps = pa_acquire(ctx)
        ctx.t("transpose", ps[0:128, 0:16], ctrash[:, m * 128:(m + 1) * 128],
              ctx.buf["IDF"][0:16, 0:16])
        ctx.mark("t"); ctx.wait("v", ctx.sem["t"])
        ev = ctx.v("tensor_copy", LW["cT"][:, m * 16:(m + 1) * 16], ps[0:128, 0:16])
        pa_release(ctx, ev)
    ctx.mark("v")
    cb = sm.ap("attT", p1=16, c0=0, c1=16)
    ctx.v("tensor_copy", cb, ctrash[:, 127::128])
    ctx.vsync()
    scex = sm.ap("attT", p1=16, c0=16, c1=32)
    ctx.v("tensor_sub", scex[:, 1:16], cb[:, 1:16], cb[:, 0:15])
    ctx.v("tensor_copy", scex[:, 0:1], cb[:, 0:1])
    ctx.vsync()
    ctx.mark("v"); ctx.wait("s", ctx.sem["v"])
    ctx.s("activation", scex, scex, AF.Exp)
    ctx.mark("s")
    ctx.wait("t", ctx.sem["v"])
    ps = pa_acquire(ctx)
    ctx.t("transpose", ps[0:16, 0:16], cb, ctx.buf["IDF"][0:16, 0:16])
    ctx.mark("t"); ctx.wait("v", ctx.sem["t"])
    cbt = sm.ap("attT", p1=16, c0=32, c1=48)
    ev = ctx.v("tensor_copy", cbt, ps[0:16, 0:16])
    pa_release(ctx, ev)
    ctx.mark("v")
    ctx.wait("q", ctx.sem["v"]); ctx.wait("q", ctx.sem["s"])
    ctx.dma("q", SC["cbtscr"], cbt)
    ctx.dma("q", SC["scscr"], scex)
    bounce_v = ctx.val(ctx.sem["qd"])
    srow = sm.ap("attT", p1=1, c0=48, c1=48 + 128)
    ctx.wait("g", ctx.sem["v"])
    ctx.wait_dma("g", "q", bounce_v)
    ctx.dma("g", srow[:, 0:128], SC["scscr"].rearrange("h m -> (h m)")[None, 0:128])
    gv = ctx.val(ctx.sem["gd"])
    ctx.wait_dma("t", "g", gv)
    for h in range(NH):
        if h == 8:
            ctx.wait("g", ctx.sem["t"])
            ctx.dma("g", srow[:, 0:128], SC["scscr"].rearrange("h m -> (h m)")[None, 128:256])
            gv = ctx.val(ctx.sem["gd"])
            ctx.wait_dma("t", "g", gv)
        ps = pa_acquire(ctx)
        ctx.t("matmul", ps[:, 0:16], sm.ap("onesrowf", p1=1),
              srow[:, (h % 8) * 16:(h % 8) * 16 + 16], start=True, stop=True)
        ctx.mark("t"); ctx.wait("v", ctx.sem["t"])
        ev = ctx.v("tensor_copy", LW["SCall"][:, h * 16:(h + 1) * 16], ps[:, 0:16])
        pa_release(ctx, ev)
    ctx.mark("v")
    for m in range(NCH):
        if m > 0:
            ctx.wait("g", ctx.sem["t"])
            ctx.dma("g", srow[:, 0:16], SC["cbtscr"][m - 1][None, :])
            ctx.dma("g", srow[:, 16:32], SC["cbtscr"][m][None, :])
        else:
            ctx.wait("g", ctx.sem["v"])
            ctx.g("memset", srow[:, 0:16], 0.0)
            ctx.mark("g")
            ctx.wait("t", ctx.sem["g"])
            ctx.dma("g", srow[:, 16:32], SC["cbtscr"][0][None, :])
        gv = ctx.val(ctx.sem["gd"])
        ctx.wait_dma("t", "g", gv)
        ctx.wait("t", ctx.sem["g"]) if False else None
        ps = pa_acquire(ctx)
        ctx.t("matmul", ps[:, 0:16], sm.ap("onesrowf", p1=1), srow[:, 0:16],
              start=True, stop=True)
        ctx.t("matmul", ps[:, 16:32], sm.ap("onesrowf", p1=1), srow[:, 16:32],
              start=True, stop=True)
        ctx.mark("t"); ctx.wait("v", ctx.sem["t"])
        tmp = sm.ap("attT", c0=64, c1=80)
        ctx.v("tensor_sub", tmp, LW["cT"][:, m * 16:(m + 1) * 16], ps[:, 0:16])
        e2 = ctx.v("tensor_tensor", LW["bAll"][:, m * 16:(m + 1) * 16],
                   ps[:, 16:32], LW["cT"][:, m * 16:(m + 1) * 16], ALU.subtract)
        pa_release(ctx, e2)
        ctx.mark("v"); ctx.wait("s", ctx.sem["v"])
        ctx.s("activation", LW["bAllpre"][:, m * 16:(m + 1) * 16] if False else tmp, tmp, AF.Exp)
        ctx.s("activation", LW["dAll"][:, m * 16:(m + 1) * 16],
              LW["bAll"][:, m * 16:(m + 1) * 16], AF.Exp)
        ctx.mark("s"); ctx.wait("v", ctx.sem["s"])
        ctx.v("tensor_copy", LW["bAll"][:, m * 16:(m + 1) * 16], tmp)
        ctx.mark("v")
    ctx.mark("v"); ctx.mark("s")


def lion_ws(ctx):
    AR = ctx.buf["AR"]
    base = 4 * T
    off = {}
    cur = base
    for nm, part, cols in [("cT", 128, 256), ("bAll", 128, 256), ("dAll", 128, 256),
                           ("SCall", 128, 256), ("Rst", 128, 520), ("Sal", 128, 520),
                           ("Asb", 128, 512), ("Dbf", 128, 512)]:
        off[nm] = (cur, part, cols)
        cur += cols
    assert cur <= 6 * T
    out = {}
    for nm, (c0, part, cols) in off.items():
        out[nm] = AR[0:part, c0:c0 + cols]
    out["Rstore"] = AR[:, off["Rst"][0]:off["Rst"][0] + 520].bitcast(BF16)
    out["Sall"] = AR[:, off["Sal"][0]:off["Sal"][0] + 520].bitcast(BF16)
    out["AsbBF"] = AR[:, off["Asb"][0]:off["Asb"][0] + 512].bitcast(BF16)
    out["DBF"] = AR[:, off["Dbf"][0]:off["Dbf"][0] + 512].bitcast(BF16)
    return out


def phi_slice(buf, h, m=None):
    p0 = 64 * (h % 2)
    c0 = (h // 2) * T
    if m is None:
        return buf[p0:p0 + 64, c0:c0 + T]
    return buf[p0:p0 + 64, c0 + m * CH: c0 + (m + 1) * CH]


def l1_lion(ctx, b):
    sm = ctx.sm
    AR = ctx.buf["AR"]
    LW = lion_ws(ctx)
    VTM = AR[:, :].bitcast(BF16)
    BUF1, BUF2, BUF3 = ctx.buf["BUF1"], ctx.buf["BUF2"], ctx.buf["BUF3"]
    ONESC = ctx.buf["ONESC"]
    SC = ctx.buf["SC"]
    PA, PB = ctx.buf["PA"], ctx.buf["PB"]
    trash = sm.ap("trash")
    attf = sm.ap("attT", c0=0, c1=65)
    attg = sm.ap("attT", c0=65, c1=130)
    tr_abs = trash[:, 0:512]
    crow = trash[0:1, 512:1024]
    ctx.barrier(("v", "s", "t"), ("v", "s", "t", "q", "g"))

    def state_mms(ctx, kt_ap, h, m, ps2, p0):
        tp = (0, p0) if p0 else None
        ctx.t("matmul", ps2[p0:p0 + 64, 0:64], kt_ap,
              VTM[:, m * 1024 + h * 64: m * 1024 + (h + 1) * 64],
              start=True, stop=True, tile_position=tp)
        ctx.t("matmul", ps2[p0:p0 + 64, 64:65], kt_ap, ONESC[:, 0:1],
              start=True, stop=True, tile_position=tp)

    for h in range(NH):
        p0 = 64 * (h % 2)
        Ssl = LW["Sall"][p0:p0 + 64, 0:65]
        Rsl = lambda m: LW["Rstore"][p0:p0 + 64, m * 65:(m + 1) * 65]
        ctx.wait("v", ctx.sem["t"])
        ctx.v("memset", Rsl(15), 0.0)
        ctx.v("memset", Ssl, 0.0)
        ctx.vsync()
        for m in range(15, 0, -1):
            pb = pbf_acquire(ctx)
            transpose_tile(ctx, pb[:, 0:64], phi_slice(BUF3, h, m))
            ctx.mark("t"); ctx.wait("v", ctx.sem["t"])
            ev = ctx.v("tensor_scalar", LW["DBF"][:, 512:576], pb[:, 0:64],
                       LW["bAll"][:, m * 16 + h: m * 16 + h + 1], None, ALU.mult)
            pbf_release(ctx, ev)
            ctx.mark("v"); ctx.wait("t", ctx.sem["v"])
            ps2 = pa_acquire(ctx)
            state_mms(ctx, LW["DBF"][:, 512:576], h, m, ps2, p0)
            ctx.mark("t"); ctx.wait("v", ctx.sem["t"])
            ev = ctx.v("scalar_tensor_tensor", Rsl(m - 1), Rsl(m),
                       LW["SCall"][p0:p0 + 64, h * 16 + m: h * 16 + m + 1],
                       ps2[p0:p0 + 64, 0:65], ALU.mult, ALU.add)
            pa_release(ctx, ev)
            ctx.vsync()
        for mg in range(4):
            ctx.wait("q", ctx.sem["v"])
            ctx.dma("q", crow[:, 0:512], SC["cscr"][h][None, mg * 512:(mg + 1) * 512])
            crv = ctx.val(ctx.sem["qd"])
            pa_drain(ctx)
            ctx.wait("t", ctx.sem["qd"], crv)
            ctx.t("matmul", PA[:, 0:512], sm.ap("onesrowf", p1=1), crow[:, 0:512],
                  start=True, stop=True)
            for j in range(4):
                m = mg * 4 + j
                ctx.t("matmul", PA[:, 512 + j * 128: 512 + (j + 1) * 128],
                      phi_slice(BUF3, h, m), phi_slice(BUF2, h, m),
                      start=True, stop=True)
            ctx.mark("t"); ctx.wait("v", ctx.sem["t"]); ctx.wait("s", ctx.sem["t"])
            cT_bc = LW["cT"][:, mg * 64 + h::16][:, 0:4].unsqueeze(2).broadcast_to([128, 4, 128])
            ctx.v("tensor_tensor", PB[:, 0:512].rearrange("p (j t) -> p j t", t=128),
                  PA[:, 0:512].rearrange("p (j t) -> p j t", t=128), cT_bc, ALU.subtract)
            ctx.mark("v"); ctx.wait("s", ctx.sem["v"])
            ctx.s("activation", tr_abs, PB[:, 0:512], AF.Abs)
            ctx.ssync()
            ctx.s("activation", LW["DBF"][:, 0:512], tr_abs, AF.Exp, scale=-1.0)
            ctx.mark("s"); ctx.wait("v", ctx.sem["s"])
            asb = ctx.v("tensor_tensor", LW["AsbBF"][:, 0:512], PA[:, 512:1024],
                        LW["DBF"][:, 0:512], ALU.mult)
            pa_manual_release(ctx, asb, 4, "v")
            ctx.mark("v"); ctx.wait("t", ctx.sem["v"])
            for j in range(4):
                m = mg * 4 + j
                bcol = LW["bAll"][:, m * 16 + h: m * 16 + h + 1]
                dcol = LW["dAll"][:, m * 16 + h: m * 16 + h + 1]
                p_a = pa_acquire(ctx)
                ctx.t("matmul", p_a[:, 0:64], LW["AsbBF"][:, j * 128:(j + 1) * 128],
                      VTM[:, m * 1024 + h * 64: m * 1024 + (h + 1) * 64],
                      start=True, stop=True)
                ctx.t("matmul", p_a[:, 64:65], LW["AsbBF"][:, j * 128:(j + 1) * 128],
                      ONESC[:, 0:1], start=True, stop=True)
                p_f = pa_acquire(ctx)
                ctx.t("matmul", p_f[:, 0:65], phi_slice(BUF2, h, m), Ssl,
                      start=True, stop=True)
                p_b = pa_acquire(ctx)
                ctx.t("matmul", p_b[:, 0:65], phi_slice(BUF2, h, m), Rsl(m),
                      start=True, stop=True)
                ctx.mark("t"); ctx.wait("v", ctx.sem["t"])
                e1 = ctx.v("tensor_scalar", attf, p_f[:, 0:65], bcol, None, ALU.mult)
                pa_release(ctx, e1)
                ctx.vsync()
                e2 = ctx.v("scalar_tensor_tensor", attg, p_b[:, 0:65], dcol, attf,
                           ALU.mult, ALU.add)
                pa_release(ctx, e2)
                ctx.vsync()
                e3 = ctx.v("tensor_tensor", attf, attg, p_a[:, 0:65], ALU.add)
                pa_release(ctx, e3)
                ctx.vsync()
                den = sm.ap("colt", c0=34, c1=35)
                ctx.v("tensor_scalar_max", den, attf[:, 64:65], 1e-6)
                ctx.vsync()
                ctx.v("reciprocal", den, den)
                ctx.vsync()
                ctx.v("tensor_scalar", BUF1[:, m * 1024 + h * 64: m * 1024 + (h + 1) * 64],
                      attf[:, 0:64], den, None, ALU.mult)
                ctx.mark("v")
                pb2 = pbf_acquire(ctx)
                ctx.wait("t", ctx.sem["v"])
                transpose_tile(ctx, pb2[:, 0:64], phi_slice(BUF3, h, m))
                ctx.mark("t"); ctx.wait("v", ctx.sem["t"])
                ev = ctx.v("tensor_scalar", LW["DBF"][:, 512:576], pb2[:, 0:64],
                           dcol, None, ALU.mult)
                pbf_release(ctx, ev)
                ctx.mark("v"); ctx.wait("t", ctx.sem["v"])
                ps2 = pa_acquire(ctx)
                state_mms(ctx, LW["DBF"][:, 512:576], h, m, ps2, p0)
                ctx.mark("t"); ctx.wait("v", ctx.sem["t"])
                ev = ctx.v("scalar_tensor_tensor", Ssl, Ssl,
                           LW["SCall"][p0:p0 + 64, h * 16 + m: h * 16 + m + 1],
                           ps2[p0:p0 + 64, 0:65], ALU.mult, ALU.add)
                pa_release(ctx, ev)
                ctx.vsync()
    ctx.mark("v"); ctx.mark("s"); ctx.mark("t")


def l1_post(ctx, b):
    sm = ctx.sm
    AR = ctx.buf["AR"]
    WB = ctx.buf["WB"]
    SC = ctx.buf["SC"]
    BUF1, BUF2, BUF3 = ctx.buf["BUF1"], ctx.buf["BUF2"], ctx.buf["BUF3"]
    trash_bf = sm.ap("trash").bitcast(BF16)
    ctx.barrier(("v", "s", "t"), ("v", "s", "t", "q", "g"))

    for src_buf, dst_buf, from_x in [(BUF1, BUF2, False), (None, BUF3, True)]:
        for tt in range(NT):
            if from_x:
                stg = trash_bf[:, (tt % 2) * 1024:(tt % 2) * 1024 + 1024]
                if tt >= 2:
                    ctx.wait("v", ctx.sem["t"])
                ctx.v("tensor_scalar", stg, xtile(ctx, tt),
                      sm.ap("lnm", c0=tt, c1=tt + 1), sm.ap("lnr", c0=tt, c1=tt + 1),
                      ALU.subtract, ALU.mult)
                ctx.mark("v"); ctx.wait("t", ctx.sem["v"])
                src_t = stg
            else:
                src_t = src_buf[:, tt * D:(tt + 1) * D]
            pb = pbf_acquire(ctx)
            for dt in range(ND):
                transpose_tile(ctx, pb[:, dt * 128:(dt + 1) * 128],
                               src_t[:, dt * 128:(dt + 1) * 128])
            ctx.mark("t"); ctx.wait("v", ctx.sem["t"])
            ev = None
            for dt in range(ND):
                ev = ctx.v("tensor_copy",
                           dst_buf[:, dt * T + tt * 128: dt * T + (tt + 1) * 128],
                           pb[:, dt * 128:(dt + 1) * 128])
            pbf_release(ctx, ev)
        ctx.mark("v")

    for dh in range(2):
        ctx.barrier(("t",), ("q",))
        ctx.dma("q", WB[:, 0:4096].rearrange("p (k c) -> p k c", c=512),
                SC["sOW"][:, dh * 512:(dh + 1) * 512].rearrange("(k dp) c -> dp k c", dp=128))
        lv = ctx.val(ctx.sem["qd"])
        ctx.wait("t", ctx.sem["qd"], lv)
        for dl in range(4):
            dt = dh * 4 + dl
            for tb in range(NTB):
                ps = pa_acquire(ctx)
                for k in range(ND):
                    ctx.t("matmul", ps, WB[:, k * 512 + dl * 128: k * 512 + (dl + 1) * 128],
                          BUF2[:, k * T + tb * TB: k * T + (tb + 1) * TB],
                          start=(k == 0), stop=(k == 7))
                ctx.mark("t"); ctx.wait("v", ctx.sem["t"])
                ev = ctx.v("scalar_tensor_tensor",
                           BUF1[:, dt * T + tb * TB: dt * T + (tb + 1) * TB],
                           ps, sm.ap("obc", c0=dt, c1=dt + 1),
                           BUF3[:, dt * T + tb * TB: dt * T + (tb + 1) * TB],
                           ALU.add, ALU.add)
                pa_release(ctx, ev)
        ctx.mark("v")

    ctx.barrier(("v", "s", "t"), ("v", "s", "t"))
    ss = sm.ap("colt", c0=16, c1=32)
    vars_ = sm.ap("colt", c0=0, c1=16)
    for tt in range(NT):
        stg = trash_bf[:, 0:1024]
        ctx.wait("t", ctx.sem["v"]); ctx.wait("t", ctx.sem["s"])
        pb = pbf_acquire(ctx)
        for dt in range(ND):
            transpose_tile(ctx, pb[:, dt * 128:(dt + 1) * 128],
                           BUF1[:, dt * T + tt * 128: dt * T + (tt + 1) * 128])
        ctx.mark("t"); ctx.wait("v", ctx.sem["t"])
        ev = ctx.v("tensor_copy", stg, pb[:, :])
        pbf_release(ctx, ev)
        ctx.v("tensor_scalar", trash_bf[:, 1024:2048], stg, 1.0 / D, 0.0,
              ALU.mult, ALU.add, accum_out=sm.ap("lnm2", c0=tt, c1=tt + 1))
        ctx.mark("v"); ctx.wait("s", ctx.sem["v"])
        ctx.s("activation", trash_bf[:, 2048:3072], stg, AF.Square,
              accum_out=ss[:, tt:tt + 1])
        ctx.mark("s")
        ctx.wait("v", ctx.sem["s"])
    ctx.v("tensor_single_scalar", vars_, ss, 1.0 / D, ALU.mult)
    ctx.v("tensor_mul", ss, sm.ap("lnm2"), sm.ap("lnm2"))
    ctx.vsync()
    ctx.v("tensor_sub", vars_, vars_, ss)
    ctx.vsync()
    ctx.mark("v"); ctx.wait("s", ctx.sem["v"])
    ctx.s("activation", vars_, vars_, AF.Sqrt, bias=sm.ap("eps1"))
    ctx.mark("s"); ctx.wait("v", ctx.sem["s"])
    ctx.v("reciprocal", sm.ap("lnr2"), vars_)
    ctx.vsync()
    for tt in range(NT):
        stg = trash_bf[:, 0:1024]
        stg2 = trash_bf[:, 1024:2048]
        ctx.wait("t", ctx.sem["v"]); ctx.wait("t", ctx.sem["s"])
        pb = pbf_acquire(ctx)
        for dt in range(ND):
            transpose_tile(ctx, pb[:, dt * 128:(dt + 1) * 128],
                           BUF1[:, dt * T + tt * 128: dt * T + (tt + 1) * 128])
        ctx.mark("t"); ctx.wait("v", ctx.sem["t"])
        ev = ctx.v("tensor_copy", stg, pb[:, :])
        pbf_release(ctx, ev)
        ctx.v("tensor_scalar", stg2, stg,
              sm.ap("lnm2", c0=tt, c1=tt + 1), sm.ap("lnr2", c0=tt, c1=tt + 1),
              ALU.subtract, ALU.mult)
        ctx.mark("v"); ctx.wait("t", ctx.sem["v"])
        pb2 = pbf_acquire(ctx)
        for dt in range(ND):
            transpose_tile(ctx, pb2[:, dt * 128:(dt + 1) * 128],
                           stg2[:, dt * 128:(dt + 1) * 128])
        ctx.mark("t"); ctx.wait("v", ctx.sem["t"])
        ev = None
        for dt in range(ND):
            ev = ctx.v("tensor_scalar",
                       BUF2[:, dt * T + tt * 128: dt * T + (tt + 1) * 128],
                       pb2[:, dt * 128:(dt + 1) * 128],
                       sm.ap("ln2w", c0=dt, c1=dt + 1),
                       sm.ap("ln2b", c0=dt, c1=dt + 1), ALU.mult, ALU.add)
        pbf_release(ctx, ev)
    ctx.mark("v")

    H1 = AR[:, :].bitcast(BF16)
    for tb in range(NTB):
        for ch in range(8):
            ctx.barrier(("t",), ("q",))
            ctx.dma("q", WB[:, 0:4096].rearrange("p (k c) -> p k c", c=512),
                    SC["sF1"][:, ch * 512:(ch + 1) * 512].rearrange("(k dp) c -> dp k c", dp=128))
            lv = ctx.val(ctx.sem["qd"])
            ctx.wait("t", ctx.sem["qd"], lv)
            for dl in range(4):
                kt = ch * 4 + dl
                ps = pa_acquire(ctx)
                for k in range(ND):
                    ctx.t("matmul", ps, WB[:, k * 512 + dl * 128: k * 512 + (dl + 1) * 128],
                          BUF2[:, k * T + tb * TB: k * T + (tb + 1) * TB],
                          start=(k == 0), stop=(k == 7))
                ctx.mark("t")
                tr = sm.ap("trash", c0=(kt % 2) * 512, c1=(kt % 2) * 512 + 512)
                ctx.wait("v", ctx.sem["t"])
                ctx.wait("v", ctx.sem["s"])
                e1 = ctx.v("tensor_scalar", tr, ps, sm.ap("f1b", c0=kt, c1=kt + 1),
                           None, ALU.add)
                pa_release(ctx, e1)
                ctx.mark("v"); ctx.wait("s", ctx.sem["v"])
                ctx.s("activation", H1[:, kt * 512:(kt + 1) * 512], tr, AF.Gelu)
                ctx.mark("s")
        ctx.wait("t", ctx.sem["s"])
        for dt in range(ND):
            ctx.barrier(("t",), ("q",))
            ctx.dma("q", WB[:, 0:4096].rearrange("p (k c) -> p k c", c=128),
                    SC["sF2"][:, dt * 128:(dt + 1) * 128].rearrange("(k dp) c -> dp k c", dp=128))
            lv = ctx.val(ctx.sem["qd"])
            ctx.wait("t", ctx.sem["qd"], lv)
            ps = pa_acquire(ctx)
            for k in range(32):
                ctx.t("matmul", ps, WB[:, k * 128:(k + 1) * 128],
                      H1[:, k * 512:(k + 1) * 512], start=(k == 0), stop=(k == 31))
            ctx.mark("t"); ctx.wait("v", ctx.sem["t"])
            dst = BUF1[:, dt * T + tb * TB: dt * T + (tb + 1) * TB]
            ev = ctx.v("scalar_tensor_tensor", dst, ps,
                       sm.ap("f2b", c0=dt, c1=dt + 1), dst, ALU.add, ALU.add)
            pa_release(ctx, ev)
        ctx.mark("v")
    ctx.mark("v"); ctx.mark("s")


def l1_ffn(ctx, b):
    sm = ctx.sm
    AR = ctx.buf["AR"]
    WB = ctx.buf["WB"]
    SC = ctx.buf["SC"]
    BUF2, BUF3 = ctx.buf["BUF2"], ctx.buf["BUF3"]
    trash = sm.ap("trash")
    ctx.barrier(("v", "s", "t"), ("v", "s", "t", "q", "g"))
    ctx.dma("q", WB[:, 0:64].rearrange("p (k c) -> p k c", c=8),
            SC["sWgT"].rearrange("(k dp) c -> dp k c", dp=128))
    ctx.dma("q", WB[0:8, 64:1088], SC["sBExp"])
    lv = ctx.val(ctx.sem["qd"])
    ctx.wait("t", ctx.sem["qd"], lv)
    pa_drain(ctx)
    PA = ctx.buf["PA"]
    for tb in range(NTB):
        for k in range(ND):
            ctx.t("matmul", PA[0:8, tb * 512:(tb + 1) * 512], WB[:, k * 8:(k + 1) * 8],
                  BUF3[:, k * T + tb * TB: k * T + (tb + 1) * TB],
                  start=(k == 0), stop=(k == 7))
    ctx.mark("t"); ctx.wait("v", ctx.sem["t"])
    gl = trash[0:8, 0:2048]
    e0 = ctx.v("tensor_scalar", gl, PA[0:8, 0:2048], sm.ap("bgate", p1=8), None, ALU.add)
    pa_manual_release(ctx, e0, 4, "v")
    ctx.vsync()
    ctx.wait("t", ctx.sem["v"])
    for tt in range(NT):
        ps = pa_acquire(ctx)
        ctx.t("transpose", ps[0:128, 0:8], gl[:, tt * 128:(tt + 1) * 128],
              ctx.buf["IDF"][0:8, 0:8])
        ctx.mark("t"); ctx.wait("v", ctx.sem["t"])
        L = sm.ap("colt", c0=36, c1=44)
        m1 = sm.ap("colt", c0=44, c1=45)
        m2 = sm.ap("colt", c0=45, c1=46)
        mk = sm.ap("colt", c0=46, c1=54)
        ev = ctx.v("tensor_copy", L, ps[0:128, 0:8])
        pa_release(ctx, ev)
        ctx.vsync()
        ctx.v("reduce_max", m1, L, axis=AX.X)
        ctx.vsync()
        ctx.v("tensor_scalar", mk, L, m1, None, ALU.is_ge)
        ctx.vsync()
        ctx.v("scalar_tensor_tensor", mk, mk, -1e30, L, ALU.mult, ALU.add)
        ctx.vsync()
        ctx.v("reduce_max", m2, mk, axis=AX.X)
        ctx.vsync()
        ctx.v("tensor_scalar", L, L, m1, None, ALU.subtract)
        ctx.vsync()
        ctx.mark("v"); ctx.wait("s", ctx.sem["v"])
        ctx.s("activation", L, L, AF.Exp)
        ctx.mark("s"); ctx.wait("v", ctx.sem["s"])
        ctx.v("tensor_scalar", mk, L, m1, m2, ALU.bypass, ALU.bypass) if False else None
        em2 = sm.ap("colt", c0=54, c1=55)
        ctx.v("tensor_sub", em2, m2, m1)
        ctx.vsync()
        ctx.mark("v"); ctx.wait("s", ctx.sem["v"])
        ctx.s("activation", em2, em2, AF.Exp)
        ctx.mark("s"); ctx.wait("v", ctx.sem["s"])
        ctx.v("tensor_scalar", mk, L, em2, None, ALU.is_ge)
        ctx.vsync()
        ctx.v("tensor_mul", L, L, mk)
        ctx.vsync()
        ssum = sm.ap("colt", c0=55, c1=56)
        ctx.v("reduce_sum", ssum, L, axis=AX.X)
        ctx.vsync()
        ctx.v("reciprocal", ssum, ssum)
        ctx.vsync()
        ctx.v("tensor_scalar", sm.ap("wtm2", c0=tt * 8, c1=(tt + 1) * 8), L, ssum,
              None, ALU.mult)
        ctx.mark("v")
    WFM2 = AR[0:8, 0:1024].bitcast(BF16)
    ctx.wait("t", ctx.sem["v"])
    for tt in range(NT):
        ps = pa_acquire(ctx)
        ctx.t("transpose", ps[0:8, 0:128], sm.ap("wtm2", c0=tt * 8, c1=(tt + 1) * 8),
              ctx.buf["IDF"][:, :])
        ctx.mark("t"); ctx.wait("v", ctx.sem["t"])
        ev = ctx.v("tensor_copy", WFM2[:, tt * 128:(tt + 1) * 128], ps[0:8, 0:128])
        pa_release(ctx, ev)
    ctx.mark("v")
    ctx.wait("t", ctx.sem["v"])
    for tt in range(NT):
        for dh in range(2):
            ps = pa_acquire(ctx)
            ctx.t("matmul", ps, WFM2[:, tt * 128:(tt + 1) * 128],
                  WB[0:8, 64 + dh * 512: 64 + (dh + 1) * 512], start=True, stop=True)
            ctx.mark("t"); ctx.wait("v", ctx.sem["t"])
            ev = ctx.v("tensor_copy",
                       BUF2[:, tt * D + dh * 512: tt * D + (dh + 1) * 512], ps)
            pa_release(ctx, ev)
    ctx.mark("v")
    for e in range(8):
        for dh in range(2):
            ctx.barrier(("t",), ("q",))
            ctx.dma("q", WB[:, 1536:5632].rearrange("p (k c) -> p k c", c=512),
                    SC["sWexpT"][e][:, dh * 512:(dh + 1) * 512].rearrange("(k dp) c -> dp k c", dp=128))
            lv = ctx.val(ctx.sem["qd"])
            ctx.wait("t", ctx.sem["qd"], lv)
            for tt in range(NT):
                ps = pa_acquire(ctx)
                for k in range(ND):
                    ctx.t("matmul", ps, BUF3[:, k * T + tt * 128: k * T + (tt + 1) * 128],
                          WB[:, 1536 + k * 512: 1536 + (k + 1) * 512],
                          start=(k == 0), stop=(k == 7))
                ctx.mark("t"); ctx.wait("v", ctx.sem["t"])
                dst = BUF2[:, tt * D + dh * 512: tt * D + (dh + 1) * 512]
                ev = ctx.v("scalar_tensor_tensor", dst, ps,
                           sm.ap("wtm2", c0=tt * 8 + e, c1=tt * 8 + e + 1), dst,
                           ALU.mult, ALU.add)
                pa_release(ctx, ev)
            ctx.mark("v")
    ctx.mark("v"); ctx.wait("s", ctx.sem["v"])
    for tt in range(NT):
        ctx.s("activation", BUF2[:, tt * D:(tt + 1) * D],
              BUF2[:, tt * D:(tt + 1) * D], AF.Silu)
    ctx.mark("s")



_CACHED = {}


def _get_nc():
    if "nc" not in _CACHED:
        _CACHED["nc"] = build(dict(nb=1, do_l0attn=True, do_l0ffn=True,
                                   do_l1=True, l1_full=True, dumps={}))
    return _CACHED["nc"]


def kernel(**inputs):
    from concourse.bass_utils import run_bass_kernel_spmd
    x = np.ascontiguousarray(inputs["L0_attn_x"]).reshape(32, T, D)
    nc = _get_nc()
    wmap = {k: np.ascontiguousarray(v) for k, v in inputs.items()
            if k != "L0_attn_x"}
    out = np.empty((32, T, D), dtype=np.float32)
    for li in range(4):
        in_maps = []
        for c in range(8):
            m = {"xin": x[c * 4 + li:c * 4 + li + 1]}
            m.update(wmap)
            in_maps.append(m)
        res = run_bass_kernel_spmd(nc, in_maps, core_ids=list(range(8)))
        for c in range(8):
            out[c * 4 + li] = res.results[c]["xout"][0]
    return out.astype(np.float32)
